# version 30
# baseline (speedup 1.0000x reference)
import numpy as np
import ml_dtypes
import concourse.bass as bass
import concourse.mybir as mybir
from concourse.alu_op_type import AluOpType
from concourse.bass_utils import run_bass_kernel_spmd
from concourse.aot_env import allow_ici_all_to_all
from contextlib import ExitStack

F32 = mybir.dt.float32
BF16 = mybir.dt.bfloat16
AF = mybir.ActivationFunctionType
AX = mybir.AxisListType
OP = AluOpType

NCORES = 8
B, T, C = 2, 4096, 2048
BT = B * T
H, DH, HALF = 8, 256, 128
HID, GV = 8192, 16384
TOK = BT // NCORES
LAMBDA_INIT = 0.8
RMS_EPS = 1.1920929e-07
LN_EPS = 1e-05
FFN_EPS = 1e-05
MAGIC = 1.5 * 2.0 ** 23
ISCALE = 0.08838834764831845
QB = 127.0

NBLK = 16
BLK = 512
QBLK = 256
_TRUNC = None


class Kb:

    QSTEP = {"pe": 1, "dve": 1, "act": 1, "pool": 1,
             "qsp": 16, "qact": 16, "qgp": 16, "qcc": 1}
    STREAM_OF = {"pe": "pe", "dve": "dve", "act": "act", "pool": "pool",
                 "qsp": "sp", "qact": "act", "qgp": "pool", "qcc": "pool"}

    POOL = {"qsp": 10, "qgp": 6, "qact": 2, "qcc": 4}

    def __init__(self, nc, st):
        self.nc = nc
        self.st = st
        self.sem = {}
        self.cnt = {}
        for name in ("pe", "dve", "act", "pool"):
            self.sem[name] = st.enter_context(nc.semaphore(name))
            self.cnt[name] = 0
        for q, k in self.POOL.items():
            for i in range(k):
                nm = f"{q}{i}"
                self.sem[nm] = st.enter_context(nc.semaphore(nm))
                self.cnt[nm] = 0
        self.rot = {q: 0 for q in self.POOL}
        self.streams = {"pe": [], "dve": [], "act": [], "pool": [], "sp": []}
        self.waited = {s: {} for s in self.streams}
        self.lastw = {}
        self.readers = {}

    def sbuf(self, name, shape, dtype=F32, scope=None):
        h = (scope or self.st).enter_context(self.nc.sbuf_tensor(name, shape, dtype))
        return h

    def psum(self, name, shape, dtype=F32, scope=None):
        h = (scope or self.st).enter_context(self.nc.psum_tensor(name, shape, dtype))
        return h

    def dram(self, name, shape, dtype=F32, kind="Internal"):
        h = self.nc.dram_tensor(name, shape, dtype, kind=kind)
        return h

    @staticmethod
    def _key(b):
        if isinstance(b, tuple):
            return b[0].name + "#" + str(b[1])
        return b.name

    def _emit_on(self, qname, sname, step, fn, reads, writes, extra_gate=None):
        stream = self.STREAM_OF[qname]
        need = {}

        def add(sem, cnt):
            if need.get(sem, -1) < cnt:
                need[sem] = cnt
        rkeys = [self._key(b) for b in reads]
        wkeys = [self._key(b) for b in writes]
        for k in rkeys:
            for sem, cnt in self.lastw.get(k, {}).items():
                add(sem, cnt)
        for k in wkeys:
            for sem, cnt in self.lastw.get(k, {}).items():
                add(sem, cnt)
            for sem, cnt in self.readers.get(k, {}).items():
                add(sem, cnt)
        if stream == "pe":
            need.pop("pe", None)
        if extra_gate:
            add(*extra_gate)
        waits = []
        for sem, cnt in sorted(need.items()):
            if self.waited[stream].get(sem, -1) < cnt:
                waits.append((sem, cnt))
                self.waited[stream][sem] = cnt
        self.cnt[sname] += step
        mark = (sname, self.cnt[sname])
        for k in wkeys:
            lw = self.lastw.setdefault(k, {})
            lw[sname] = mark[1]
            self.readers[k] = {}
        for k in rkeys:
            rd = self.readers.setdefault(k, {})
            if rd.get(sname, -1) < mark[1]:
                rd[sname] = mark[1]
        sem_h = self.sem[sname]
        semmap = self.sem

        def closure(eng, waits=tuple(waits), fn=fn, sem_h=sem_h, step=step):
            for s, c in waits:
                eng.wait_ge(semmap[s], c)
            fn(eng).then_inc(sem_h, step)
        self.streams[stream].append(closure)
        return mark

    def emit(self, qname, fn, reads=(), writes=()):
        if qname in self.POOL:
            k = self.POOL[qname]
            i = self.rot[qname]
            self.rot[qname] = (i + 1) % k
            sname = f"{qname}{i}"
            prev = self.cnt[sname]
            gate = (sname, prev) if prev else None
            step = self.QSTEP[qname]
            return self._emit_on(qname, sname, step, fn, reads, writes,
                                 extra_gate=gate)
        return self._emit_on(qname, qname, self.QSTEP[qname], fn, reads, writes)

    def dma(self, qname, out_ap, in_ap, reads=(), writes=()):
        self.emit(qname, lambda eng, o=out_ap, i=in_ap: eng.dma_start(out=o, in_=i),
                  reads=reads, writes=writes)

    def wait_all(self, stream, exclude=()):
        for s in self.cnt:
            if any(s.startswith(e) for e in exclude):
                continue
            c = self.cnt[s]
            if c and self.waited[stream].get(s, -1) < c:
                self.waited[stream][s] = c
                self.streams[stream].append(
                    lambda eng, s=s, c=c: eng.wait_ge(self.sem[s], c))

    def barrier(self):
        for stream in self.streams:
            self.wait_all(stream, exclude=("qcc",))

    def finalize(self):
        if _TRUNC is not None:
            for k in self.streams:
                f = _TRUNC.get(k, 1.0) if isinstance(_TRUNC, dict) else _TRUNC
                n = int(len(self.streams[k]) * f) if f <= 1.0 else int(f)
                self.streams[k] = self.streams[k][:n]
        nc = self.nc
        with nc.Block() as block:
            @block.tensor
            def _(t):
                for f in self.streams["pe"]:
                    f(t)

            @block.vector
            def _(v):
                for f in self.streams["dve"]:
                    f(v)

            @block.scalar
            def _(s):
                for f in self.streams["act"]:
                    f(s)

            @block.gpsimd
            def _(g):
                for f in self.streams["pool"]:
                    f(g)

            @block.sync
            def _(sy):
                for f in self.streams["sp"]:
                    f(sy)


def brow(ap, parts=128):
    return ap.partition_broadcast(parts)


def build_program(debug_taps=()):
    taps = list(debug_taps)
    nc = bass.Bass("TRN2", target_bir_lowering=False, debug=False)

    def ext_in(name, shape, dtype=F32):
        h = nc.dram_tensor(name, shape, dtype, kind="ExternalInput")
        return h

    x = ext_in("x", [BT, C])
    cid_in = nc.dram_tensor("cid", [1, 2], mybir.dt.int32, kind="ExternalInput")
    x_mine = ext_in("x_mine", [TOK, C])
    wq = ext_in("wq", [DH, C])
    wk = ext_in("wk", [DH, C])
    wv = ext_in("wv", [DH, C])
    wo = ext_in("wo", [C, C])
    lam_in = ext_in("lam_in", [1, 4 * DH])
    nw = ext_in("nw", [1, C])
    wgv_sl = ext_in("wgv_sl", [GV, DH])
    wout_sl = ext_in("wout_sl", [C, TOK])
    out = nc.dram_tensor("out", [TOK, C], F32, kind="ExternalOutput")

    with ExitStack() as st:
        kb = Kb(nc, st)

        cc_y_in = [kb.dram(f"cc_y_in{s}", [DH, T], BF16) for s in range(2)]
        cc_y_ag = nc.dram_tensor("cc_y_ag", [2 * C, T], BF16, addr_space="Shared")
        cc_gv_in = kb.dram("cc_gv_in", [DH, GV], BF16)
        cc_gv_out = nc.dram_tensor("cc_gv_out", [C, GV], BF16, addr_space="Shared")
        cc_o_in = kb.dram("cc_o_in", [TOK, C], BF16)
        wo_t_dram = kb.dram("wo_t", [C, C], BF16)
        cc_o_out = nc.dram_tensor("cc_o_out", [HID, C], BF16, addr_space="Shared")
        cc_s_in = kb.dram("cc_s_in", [1, 2], F32)
        cc_s_out = kb.dram("cc_s_out", [1, 2], F32)
        h2_dram = kb.dram("h2_dram", [TOK, HID], F32)
        r2_dram = kb.dram("r2_dram", [TOK, C], F32)

        def tap(name, src_ap, shape, dtype=F32, reads=()):
            if name not in taps:
                return
            t = nc.dram_tensor("tap_" + name, shape, dtype, kind="ExternalOutput")
            kb.dma("qsp", t.ap(), src_ap, reads=reads, writes=(t,))

        identf = kb.sbuf("identf", [128, 128], F32)
        identb = kb.sbuf("identb", [128, 128], BF16)
        ones_colb = kb.sbuf("ones_colb", [128, 1], BF16)
        onesf_row = kb.sbuf("onesf_row", [1, 128], F32)
        neglam = kb.sbuf("neglam", [1, 1], F32)
        s_pair = kb.sbuf("s_pair", [1, 4], F32)
        invs = kb.sbuf("invs", [1, 2], F32)
        sgv127_col = kb.sbuf("sgv127_col", [128, 1], F32)
        sout127_col = kb.sbuf("sout127_col", [128, 1], F32)
        invs_gv_col = kb.sbuf("invs_gv_col", [128, 1], F32)
        invs_o_col = kb.sbuf("invs_o_col", [128, 1], F32)
        alph1 = kb.sbuf("alph1", [128, 8], F32)
        alph2 = kb.sbuf("alph2", [128, 8], F32)
        m2acc = kb.sbuf("m2acc", [128, 8], F32)
        m2s = kb.sbuf("m2s", [128, 8], F32)

        st_iota = st.enter_context(ExitStack())
        iotat = kb.sbuf("iotat", [128, 512], F32, scope=st_iota)
        lam_sb = kb.sbuf("lam_sb", [1, 4 * DH], F32, scope=st_iota)
        lamv = kb.sbuf("lamv", [1, 2 * DH], F32, scope=st_iota)
        lam_red = kb.sbuf("lam_red", [1, 4], F32, scope=st_iota)
        kb.emit("pool", lambda g: g.iota(iotat[:, :], pattern=[[1, 512]], base=0,
                                         channel_multiplier=-1,
                                         allow_small_or_imprecise_dtypes=True),
                writes=(iotat,))
        kb.emit("dve", lambda v: v.tensor_single_scalar(
            out=identf[:, :], in_=iotat[:, 0:128], scalar=0.0, op=OP.is_equal),
            reads=(iotat,), writes=(identf,))
        kb.emit("dve", lambda v: v.tensor_copy(out=identb[:, :], in_=identf[:, :]),
                reads=(identf,), writes=(identb,))
        kb.emit("dve", lambda v: v.memset(ones_colb[:, :], 1.0), writes=(ones_colb,))
        kb.emit("dve", lambda v: v.memset(onesf_row[:, :], 1.0), writes=(onesf_row,))
        kb.dma("qsp", lam_sb[:, :], lam_in.ap(), reads=(lam_in,), writes=(lam_sb,))
        kb.emit("dve", lambda v: v.tensor_tensor(
            out=lamv[0:1, 0:DH], in0=lam_sb[0:1, 0:DH],
            in1=lam_sb[0:1, DH:2 * DH], op=OP.mult),
            reads=(lam_sb,), writes=(lamv,))
        kb.emit("dve", lambda v: v.tensor_tensor(
            out=lamv[0:1, DH:2 * DH], in0=lam_sb[0:1, 2 * DH:3 * DH],
            in1=lam_sb[0:1, 3 * DH:4 * DH], op=OP.mult),
            reads=(lam_sb,), writes=(lamv,))
        kb.emit("dve", lambda v: v.reduce_sum(out=lam_red[0:1, 0:1],
                                              in_=lamv[0:1, 0:DH], axis=AX.X),
                reads=(lamv,), writes=(lam_red,))
        kb.emit("dve", lambda v: v.reduce_sum(out=lam_red[0:1, 1:2],
                                              in_=lamv[0:1, DH:2 * DH], axis=AX.X),
                reads=(lamv,), writes=(lam_red,))
        kb.emit("act", lambda s: s.activation(out=lam_red[0:1, 2:4],
                                              in_=lam_red[0:1, 0:2], func=AF.Exp),
                reads=(lam_red,), writes=(lam_red,))
        kb.emit("dve", lambda v: v.tensor_scalar(
            out=neglam[:, :], in0=lam_red[0:1, 3:4], scalar1=lam_red[0:1, 2:3],
            scalar2=-LAMBDA_INIT, op0=OP.subtract, op1=OP.add),
            reads=(lam_red,), writes=(neglam,))
        st_iota.close()

        wout_r = wout_sl.ap().rearrange("(g p) n -> g p n", p=128)

        def wgv_src(g):
            return wgv_sl.ap()[g * 1024:(g + 1) * 1024, :].rearrange(
                "(c p) n -> p c n", p=128)

        def x_src(blk, half):
            r0 = blk * BLK + half * 256
            return x.ap()[r0:r0 + 256, :].rearrange("(c p) n -> p c n", p=128)

        st_qa = st.enter_context(ExitStack())
        q1T = kb.sbuf("q1T", [128, BT], BF16, scope=st_qa)
        q2T = kb.sbuf("q2T", [128, BT], BF16, scope=st_qa)
        k1T = kb.sbuf("k1T", [128, BT], BF16, scope=st_qa)
        k2T = kb.sbuf("k2T", [128, BT], BF16, scope=st_qa)
        vj = kb.sbuf("vj", [128, 64 * 257], BF16, scope=st_qa)
        kb.emit("dve", lambda v: v.memset(vj[:, :], 1.0), writes=(vj,))

        wqT = kb.sbuf("wqT", [128, 16 * DH], BF16, scope=st_qa)
        wkT = kb.sbuf("wkT", [128, 16 * DH], BF16, scope=st_qa)
        wvT = kb.sbuf("wvT", [128, 16 * DH], BF16, scope=st_qa)
        negrs = kb.sbuf("negrs", [1, 3 * DH], BF16, scope=st_qa)

        with ExitStack() as st0:
            wstage = [kb.sbuf(f"wstage{i}", [128, C], F32, scope=st0) for i in range(2)]
            wstage_b = [kb.sbuf(f"wstage_b{i}", [128, C], BF16, scope=st0)
                        for i in range(2)]
            wrs_col = [kb.sbuf(f"wrs_col{i}", [128, 1], F32, scope=st0)
                       for i in range(2)]
            ps_rows = kb.psum("ps_rows", [128, 512], F32, scope=st0)
            ps_tw0 = [kb.psum(f"ps_tw0_{i}", [128, 1024], BF16, scope=st0)
                      for i in range(2)]
            wg_a = [kb.sbuf(f"wg_a{i}", [128, 2048], F32, scope=st0)
                    for i in range(2)]
            acc2 = kb.sbuf("acc2", [128, 2], F32, scope=st0)
            tmp_col = kb.sbuf("tmp_col", [128, 1], F32, scope=st0)
            kb.emit("dve", lambda v: v.memset(acc2[:, :], 0.0), writes=(acc2,))

            wpa_jobs = [(srcsel, g) for g in range(16) for srcsel in (0, 1)]

            def wpa_group(idx):
                srcsel, g = wpa_jobs[idx]
                col = srcsel
                w = 2048 if srcsel == 0 else 1024
                buf = wg_a[idx % 2]
                if srcsel == 0:
                    kb.dma("qsp", buf[:, 0:w].rearrange("p (c n) -> p c n", c=8),
                           wgv_src(g), reads=(wgv_sl,), writes=(buf,))
                else:
                    kb.dma("qsp", buf[:, 0:w], wout_r[g, :, :],
                           reads=(wout_sl,), writes=(buf,))
                kb.emit("dve", lambda v, w=w, buf=buf: v.reduce_sum(
                    out=tmp_col[:, :], in_=buf[:, 0:w], axis=AX.X,
                    apply_absolute_value=True),
                    reads=(buf,), writes=(tmp_col,))
                kb.emit("dve", lambda v, col=col: v.tensor_tensor(
                    out=acc2[:, col:col + 1], in0=acc2[:, col:col + 1],
                    in1=tmp_col[:, 0:1], op=OP.add),
                    reads=(acc2, tmp_col), writes=(acc2,))

            wpa_i = [0]

            def emit_wpa(n):
                for _ in range(n):
                    if wpa_i[0] < len(wpa_jobs):
                        wpa_group(wpa_i[0])
                        wpa_i[0] += 1

            stage = 0
            for wi, (wsrc, wdst) in enumerate(((wq, wqT), (wk, wkT), (wv, wvT))):
                for hf in range(2):
                    sp = stage % 2
                    kb.dma("qsp", wstage[sp][:, :],
                           wsrc.ap()[hf * 128:(hf + 1) * 128, :],
                           reads=(wsrc,), writes=(wstage[sp],))
                    kb.emit("dve", lambda v, sp=sp: v.reduce_sum(
                        out=wrs_col[sp][:, :], in_=wstage[sp][:, :], axis=AX.X),
                        reads=(wstage[sp],), writes=(wrs_col[sp],))
                    kb.emit("pe", lambda t, sp=sp: t.matmul(
                        ps_rows[0:1, 0:128], wrs_col[sp][:, 0:1], identf[:, :],
                        start=True, stop=True),
                        reads=(wrs_col[sp], identf), writes=(ps_rows,))
                    dst0 = wi * DH + hf * 128
                    kb.emit("act", lambda s, dst0=dst0: s.activation(
                        out=negrs[:, dst0:dst0 + 128], in_=ps_rows[0:1, 0:128],
                        func=AF.Copy, scale=-1.0),
                        reads=(ps_rows,), writes=(negrs,))
                    kb.emit("dve", lambda v, sp=sp: v.tensor_copy(
                        out=wstage_b[sp][:, :], in_=wstage[sp][:, :]),
                        reads=(wstage[sp],), writes=(wstage_b[sp],))
                    for i8 in range(2):
                        ptw = ps_tw0[i8]
                        for j in range(8):
                            i = i8 * 8 + j
                            kb.emit("pe", lambda t, i=i, j=j, sp=sp, ptw=ptw: t.transpose(
                                ptw[:, j * 128:(j + 1) * 128],
                                wstage_b[sp][:, i * 128:(i + 1) * 128], identb[:, :]),
                                reads=(wstage_b[sp], identb), writes=(ptw,))
                        if wi < 2:
                            d0 = hf * 2048 + i8 * 8 * 128
                            kb.emit("dve", lambda v, d0=d0, wdst=wdst, ptw=ptw: v.tensor_copy(
                                out=wdst[:, d0:d0 + 1024], in_=ptw[:, :]),
                                reads=(ptw,), writes=(wdst,))
                        else:
                            for j in range(8):
                                i = i8 * 8 + j
                                d0 = i * 256 + hf * 128
                                kb.emit("dve", lambda v, d0=d0, j=j, ptw=ptw: v.tensor_copy(
                                    out=wvT[:, d0:d0 + 128],
                                    in_=ptw[:, j * 128:(j + 1) * 128]),
                                    reads=(ptw,), writes=(wvT,))
                    emit_wpa(6)
                    stage += 1
            emit_wpa(len(wpa_jobs))
            for col in range(2):
                kb.emit("pe", lambda t, col=col: t.matmul(
                    ps_rows[0:1, 0:128], acc2[:, col:col + 1], identf[:, :],
                    start=True, stop=True),
                    reads=(acc2, identf), writes=(ps_rows,))
                kb.emit("dve", lambda v: v.tensor_copy(out=wg_a[0][0:1, 0:128],
                                                       in_=ps_rows[0:1, 0:128]),
                        reads=(ps_rows,), writes=(wg_a[0],))
                kb.emit("dve", lambda v, col=col: v.reduce_sum(
                    out=s_pair[0:1, col:col + 1], in_=wg_a[0][0:1, 0:128], axis=AX.X),
                    reads=(wg_a[0],), writes=(s_pair,))
            kb.dma("qgp", cc_s_in.ap(), s_pair[0:1, 0:2], reads=(s_pair,),
                   writes=(cc_s_in,))
            kb.emit("qcc", lambda g: g.collective_compute(
                "AllReduce", OP.add, replica_groups=[list(range(NCORES))],
                ins=[cc_s_in.ap()], outs=[cc_s_out.ap()]),
                reads=(cc_s_in,), writes=(cc_s_out,))

        kb.barrier()
        with ExitStack() as stq:
            xbf = [kb.sbuf(f"xbf{i}", [128, 2 * C], BF16, scope=stq) for i in range(3)]
            xTb = kb.sbuf("xTb", [128, 16 * BLK], BF16, scope=stq)
            sqs = kb.sbuf("sqs", [128, C], BF16, scope=stq)
            qsq = [kb.sbuf(f"qsq{i}", [128, BLK], BF16, scope=stq) for i in range(2)]
            mu4 = [kb.sbuf(f"mu4_{i}", [128, 4], F32, scope=stq) for i in range(2)]
            mq4 = [kb.sbuf(f"mq4_{i}", [128, 4], F32, scope=stq) for i in range(2)]
            mu4b = [kb.sbuf(f"mu4b_{i}", [128, 4], BF16, scope=stq) for i in range(2)]
            r4 = [kb.sbuf(f"r4_{i}", [128, 4], F32, scope=stq) for i in range(2)]
            scol = [kb.sbuf(f"scol{i}", [128, 1], F32, scope=stq) for i in range(4)]
            murow = [kb.sbuf(f"murow{i}", [1, BLK], BF16, scope=stq) for i in range(2)]
            rrow = [kb.sbuf(f"rrow{i}", [1, BLK], F32, scope=stq) for i in range(2)]
            rrow2 = [kb.sbuf(f"rrow2_{i}", [1, BLK], F32, scope=stq) for i in range(2)]
            rrb = [kb.sbuf(f"rrb{i}", [128, BLK], F32, scope=stq) for i in range(2)]
            wg_ldh = [kb.sbuf(f"wg_ldh{r}", [128, 512], F32, scope=stq)
                      for r in range(5)]
            wg_tnh = [kb.sbuf(f"wg_tnh{r}", [128, 512], BF16, scope=stq)
                      for r in range(5)]
            tern_sth = [kb.sbuf(f"tern_sth{r}", [128, 512], BF16, scope=stq)
                        for r in range(2)]

            ps_q = [kb.psum(f"ps_q{i}", [128, 512], F32, scope=stq) for i in range(4)]
            ps_misc = [kb.psum(f"ps_misc{i}", [128, 512], F32, scope=stq)
                       for i in range(2)]
            ps_xt = [kb.psum(f"ps_xt{i}", [128, 1024], BF16, scope=stq)
                     for i in range(2)]

            def s_scalar_chain():
                kb.dma("qsp", s_pair[0:1, 2:4], cc_s_out.ap(),
                       reads=(cc_s_out,), writes=(s_pair,))
                for j, ntot in ((0, GV * C), (1, C * HID)):
                    kb.emit("dve", lambda v, j=j, nt=ntot: v.tensor_scalar(
                        out=s_pair[0:1, j:j + 1], in0=s_pair[0:1, 2 + j:3 + j],
                        scalar1=1.0 / nt, scalar2=1e-5, op0=OP.mult, op1=OP.max),
                        reads=(s_pair,), writes=(s_pair,))
                    kb.emit("dve", lambda v, j=j: v.reciprocal(
                        out=invs[0:1, j:j + 1], in_=s_pair[0:1, j:j + 1]),
                        reads=(s_pair,), writes=(invs,))
                specs = [(invs_gv_col, invs, 0, 1.0), (invs_o_col, invs, 1, 1.0),
                         (sgv127_col, s_pair, 0, 1.0 / QB),
                         (sout127_col, s_pair, 1, 1.0 / QB)]
                for dst, srcb, colj, scl in specs:
                    kb.emit("pe", lambda t, srcb=srcb, colj=colj: t.matmul(
                        ps_misc[1][:, 0:1], onesf_row[:, :], srcb[0:1, colj:colj + 1],
                        start=True, stop=True),
                        reads=(onesf_row, srcb), writes=(ps_misc[1],))
                    kb.emit("act", lambda s, dst=dst, scl=scl: s.activation(
                        out=dst[:, :], in_=ps_misc[1][:, 0:1], func=AF.Copy, scale=scl),
                        reads=(ps_misc[1],), writes=(dst,))

            _wo_slots = [("wo", j, q) for j in range(16) for q in range(4)]
            _gvo = ([("gv", gh, hf) for gh in range(32) for hf in range(2)]
                    + [("o", i, j8) for i in range(16) for j8 in range(2)])
            TSLOTS = list(_wo_slots[:16])
            _wi, _gi = 16, 0
            for _b in range(12):
                for _idx in range(8):
                    TSLOTS.append(_gvo[_gi])
                    _gi += 1
                    if _idx % 2 == 1 and _wi < 64:
                        TSLOTS.append(_wo_slots[_wi])
                        _wi += 1
            assert len(TSLOTS) == 160 and _wi == 64 and _gi == 96

            def tern_prep(k):
                kind, a, b2 = TSLOTS[k]
                r = k % 5
                if kind == "wo":
                    kb.dma("qsp", wg_ldh[r][:, :],
                           wo.ap()[a * 128:(a + 1) * 128,
                                   b2 * 512:(b2 + 1) * 512],
                           reads=(wo,), writes=(wg_ldh[r],))
                    kb.emit("dve", lambda v, r=r: v.tensor_copy(
                        out=wg_tnh[r][:, :], in_=wg_ldh[r][:, :]),
                        reads=(wg_ldh[r],), writes=(wg_tnh[r],))
                    return
                if kind == "gv":
                    src = wgv_sl.ap()[a * 512:(a + 1) * 512,
                                      b2 * 128:(b2 + 1) * 128].rearrange(
                        "(c p) n -> p c n", p=128)
                    dst = wg_ldh[r][:, :].rearrange("p (c n) -> p c n", c=4)
                    rd, scol_ = wgv_sl, invs_gv_col
                else:
                    src = wout_r[a, :, b2 * 512:(b2 + 1) * 512]
                    dst = wg_ldh[r][:, :]
                    rd, scol_ = wout_sl, invs_o_col
                kb.dma("qsp", dst, src, reads=(rd,), writes=(wg_ldh[r],))
                kb.emit("act", lambda s, r=r, scol_=scol_: s.activation(
                    out=wg_ldh[r][:, :], in_=wg_ldh[r][:, :], func=AF.Copy,
                    scale=scol_[:, 0:1], bias=MAGIC),
                    reads=(wg_ldh[r], scol_), writes=(wg_ldh[r],))
                kb.emit("dve", lambda v, r=r: v.tensor_scalar(
                    out=wg_ldh[r][:, :], in0=wg_ldh[r][:, :], scalar1=-MAGIC,
                    scalar2=1.0, op0=OP.add, op1=OP.min),
                    reads=(wg_ldh[r],), writes=(wg_ldh[r],))
                kb.emit("dve", lambda v, r=r: v.tensor_scalar_max(
                    out=wg_tnh[r][:, :], in0=wg_ldh[r][:, :], scalar1=-1.0),
                    reads=(wg_ldh[r],), writes=(wg_tnh[r],))

            def tern_trans(k):
                kind, a, b2 = TSLOTS[k]
                r = k % 5
                pr = k % 2
                pxs = ps_xt[k % 2]
                for cseg in range(4):
                    kb.emit("pe", lambda t, cseg=cseg, r=r, pxs=pxs: t.transpose(
                        pxs[:, cseg * 128:(cseg + 1) * 128],
                        wg_tnh[r][:, cseg * 128:(cseg + 1) * 128],
                        identb[:, :]),
                        reads=(wg_tnh[r], identb), writes=(pxs,))
                kb.emit("act", lambda s, pr=pr, pxs=pxs: s.activation(
                    out=tern_sth[pr][:, :], in_=pxs[:, 0:512],
                    func=AF.Copy),
                    reads=(pxs,), writes=(tern_sth[pr],))
                if kind == "wo":
                    dstw = wo_t_dram.ap()[b2 * 512:(b2 + 1) * 512,
                                          a * 128:(a + 1) * 128].rearrange(
                        "(c p) n -> p c n", p=128)
                    kb.dma("qact", dstw,
                           tern_sth[pr][:, :].rearrange("p (c n) -> p c n", c=4),
                           reads=(tern_sth[pr],),
                           writes=((wo_t_dram, (a, b2)),))
                elif kind == "gv":
                    kb.dma("qact",
                           cc_gv_in.ap()[b2 * 128:(b2 + 1) * 128,
                                         a * 512:(a + 1) * 512],
                           tern_sth[pr][:, :],
                           reads=(tern_sth[pr],), writes=((cc_gv_in, (a, b2)),))
                else:
                    dsto = cc_o_in.ap()[b2 * 512:(b2 + 1) * 512,
                                        a * 128:(a + 1) * 128].rearrange(
                        "(c p) n -> p c n", p=128)
                    kb.dma("qact", dsto,
                           tern_sth[pr][:, :].rearrange("p (c n) -> p c n", c=4),
                           reads=(tern_sth[pr],), writes=((cc_o_in, (a, b2)),))

            def xload(blk, halves=(0, 1)):
                for half in halves:
                    xb = xbf[(blk * 2 + half) % 3]
                    kb.dma("qgp", xb[:, :].rearrange("p (c n) -> p c n", c=2),
                           x_src(blk, half), reads=(x,), writes=(xb,))

            def q_block(blk):
                base = blk * BLK
                par = blk % 2
                if blk + 1 < NBLK:
                    xload(blk + 1, halves=(0,))

                def xb_at(seg):
                    return xbf[(blk * 2 + seg // 2) % 3], (seg % 2) * C
                for i2 in range(8):
                    pxt = ps_xt[i2 % 2]
                    for j in range(2):
                        i = i2 * 2 + j
                        for seg in range(4):
                            xb, c0 = xb_at(seg)
                            kb.emit("pe", lambda t, i=i, j=j, seg=seg, xb=xb,
                                    c0=c0, pxt=pxt: t.transpose(
                                pxt[:, j * 512 + seg * 128:j * 512 + (seg + 1) * 128],
                                xb[:, c0 + i * 128:c0 + (i + 1) * 128],
                                identb[:, :]),
                                reads=(xb, identb), writes=(pxt,))
                    if i2 % 2 == 0:
                        kb.emit("dve", lambda v, i2=i2, pxt=pxt: v.tensor_copy(
                            out=xTb[:, i2 * 1024:(i2 + 1) * 1024], in_=pxt[:, :]),
                            reads=(pxt,), writes=(xTb,))
                    else:
                        kb.emit("act", lambda s, i2=i2, pxt=pxt: s.activation(
                            out=xTb[:, i2 * 1024:(i2 + 1) * 1024], in_=pxt[:, :],
                            func=AF.Copy),
                            reads=(pxt,), writes=(xTb,))
                for seg in range(4):
                    xb, c0 = xb_at(seg)
                    if seg < 2:
                        kb.emit("dve", lambda v, xb=xb, c0=c0, seg=seg,
                                par=par: v.reduce_sum(
                            out=mu4[par][:, seg:seg + 1], in_=xb[:, c0:c0 + C],
                            axis=AX.X),
                            reads=(xb,), writes=(mu4[par],))
                    else:
                        kb.emit("act", lambda s, xb=xb, c0=c0, seg=seg,
                                par=par: s.activation(
                            out=sqs[:, :], in_=xb[:, c0:c0 + C], func=AF.Copy,
                            accum_out=mu4[par][:, seg:seg + 1]),
                            reads=(xb,), writes=(sqs, mu4[par]))
                kb.emit("dve", lambda v, par=par: v.tensor_scalar_mul(
                    out=mu4[par][:, :], in0=mu4[par][:, :], scalar1=1.0 / C),
                    reads=(mu4[par],), writes=(mu4[par],))
                kb.emit("dve", lambda v, par=par: v.tensor_copy(
                    out=mu4b[par][:, :], in_=mu4[par][:, :]),
                    reads=(mu4[par],), writes=(mu4b[par],))
                kb.emit("dve", lambda v, par=par: v.tensor_tensor(
                    out=mq4[par][:, :], in0=mu4[par][:, :], in1=mu4[par][:, :],
                    op=OP.mult),
                    reads=(mu4[par],), writes=(mq4[par],))

                def sq_stat(seg):
                    xb, c0 = xb_at(seg)
                    kb.emit("act", lambda s, xb=xb, c0=c0, seg=seg,
                            par=par: s.activation(
                        out=sqs[:, :], in_=xb[:, c0:c0 + C],
                        func=AF.Square, accum_out=r4[par][:, seg:seg + 1]),
                        reads=(xb,), writes=(sqs, r4[par]))

                def r4_chain():
                    kb.emit("dve", lambda v, par=par: v.scalar_tensor_tensor(
                        out=r4[par][:, :], in0=r4[par][:, :], scalar=1.0 / C,
                        in1=mq4[par][:, :], op0=OP.mult, op1=OP.subtract),
                        reads=(r4[par], mq4[par]), writes=(r4[par],))
                    kb.emit("dve", lambda v, par=par: v.tensor_scalar_add(
                        out=r4[par][:, :], in0=r4[par][:, :], scalar1=LN_EPS),
                        reads=(r4[par],), writes=(r4[par],))
                    kb.emit("dve", lambda v, par=par: v.reciprocal(
                        out=mq4[par][:, :], in_=r4[par][:, :]),
                        reads=(r4[par],), writes=(mq4[par],))
                    kb.emit("act", lambda s, par=par: s.activation(
                        out=r4[par][:, :], in_=mq4[par][:, :], func=AF.Sqrt),
                        reads=(mq4[par],), writes=(r4[par],))
                def murow_mms():
                    for seg in range(4):
                        kb.emit("pe", lambda t, seg=seg, par=par: t.matmul(
                            ps_misc[0][0:1, seg * 128:(seg + 1) * 128],
                            mu4b[par][:, seg:seg + 1], identb[:, :],
                            start=True, stop=True),
                            reads=(mu4b[par], identb), writes=(ps_misc[0],))
                    kb.emit("dve", lambda v, par=par: v.tensor_copy(
                        out=murow[par][:, :], in_=ps_misc[0][0:1, 0:512]),
                        reads=(ps_misc[0],), writes=(murow[par],))

                mats = [(0, 0, ps_q[0], q1T), (0, 1, ps_q[1], q2T),
                        (1, 0, ps_q[2], k1T), (1, 1, ps_q[3], k2T)]

                def mats_mm(m):
                    wi, hf, ps, dstT = mats[m]
                    wT = wqT if wi == 0 else wkT
                    for i in range(16):
                        kb.emit("pe", lambda t, wT=wT, hf=hf, i=i, ps=ps: t.matmul(
                            ps[:, :],
                            wT[:, hf * 2048 + i * 128:hf * 2048 + (i + 1) * 128],
                            xTb[:, i * BLK:(i + 1) * BLK],
                            start=(i == 0), stop=False),
                            reads=(wT, xTb), writes=(ps,))

                def rank1(m):
                    wi, hf, ps, dstT = mats[m]
                    rs0 = wi * DH + hf * 128
                    kb.emit("pe", lambda t, rs0=rs0, ps=ps, par=par: t.matmul(
                        ps[:, :], negrs[0:1, rs0:rs0 + 128], murow[par][:, :],
                        start=False, stop=True),
                        reads=(negrs, murow[par]), writes=(ps,))

                def post_a_sq(m):
                    _, _, ps, _ = mats[m]
                    mp = m % 2
                    kb.emit("act", lambda s, ps=ps, mp=mp: s.activation(
                        out=qsq[mp][:, :], in_=ps[:, :], func=AF.Square),
                        reads=(ps,), writes=(qsq[mp],))

                def post_a(m):
                    mp = m % 2
                    kb.emit("pe", lambda t, mp=mp: t.matmul(
                        ps_misc[1][32:33, 0:512], ones_colb[:, 0:1], qsq[mp][:, :],
                        start=True, stop=True),
                        reads=(ones_colb, qsq[mp]), writes=(ps_misc[1],))
                    kb.emit("dve", lambda v, mp=mp: v.tensor_scalar(
                        out=rrow[mp][:, :], in0=ps_misc[1][32:33, 0:512],
                        scalar1=1.0 / HALF, scalar2=RMS_EPS,
                        op0=OP.mult, op1=OP.add),
                        reads=(ps_misc[1],), writes=(rrow[mp],))
                    kb.emit("dve", lambda v, mp=mp: v.reciprocal(
                        out=rrow2[mp][:, :], in_=rrow[mp][:, :]),
                        reads=(rrow[mp],), writes=(rrow2[mp],))
                    kb.emit("act", lambda s, mp=mp: s.activation(
                        out=rrow[mp][:, :], in_=rrow2[mp][:, :], func=AF.Sqrt),
                        reads=(rrow2[mp],), writes=(rrow[mp],))

                def post_b(m):
                    _, _, ps, dstT = mats[m]
                    mp = m % 2
                    kb.emit("pe", lambda t, mp=mp: t.matmul(
                        ps_misc[0][:, 0:512], onesf_row[:, :], rrow[mp][0:1, :],
                        start=True, stop=True),
                        reads=(onesf_row, rrow[mp]), writes=(ps_misc[0],))
                    kb.emit("act", lambda s, mp=mp: s.activation(
                        out=rrb[mp][:, :], in_=ps_misc[0][:, 0:512], func=AF.Copy),
                        reads=(ps_misc[0],), writes=(rrb[mp],))
                    kb.emit("dve", lambda v, ps=ps, dstT=dstT, base=base,
                            mp=mp: v.tensor_tensor(
                        out=dstT[:, base:base + BLK], in0=ps[:, :],
                        in1=rrb[mp][:, :], op=OP.mult),
                        reads=(ps, rrb[mp]), writes=(dstT,))

                def v_mm(tt):
                    ps = ps_q[tt]
                    for i in range(16):
                        kb.emit("pe", lambda t, tt=tt, i=i, ps=ps: t.matmul(
                            ps[:, 0:256],
                            xTb[:, i * BLK + tt * 128:i * BLK + (tt + 1) * 128],
                            wvT[:, i * 256:(i + 1) * 256],
                            start=(i == 0), stop=False),
                            reads=(xTb, wvT), writes=(ps,))
                    kb.emit("pe", lambda t, tt=tt, ps=ps, par=par: t.matmul(
                        ps[:, 0:256], murow[par][0:1, tt * 128:(tt + 1) * 128],
                        negrs[0:1, 2 * DH:2 * DH + 256],
                        start=False, stop=True),
                        reads=(murow[par], negrs), writes=(ps,))
                    gidx = blk * 4 + tt
                    kb.emit("dve", lambda v, tt=tt, ps=ps, gidx=gidx,
                            par=par: v.tensor_scalar_mul(
                        out=vj[:, gidx * 257:gidx * 257 + 256], in0=ps[:, 0:256],
                        scalar1=r4[par][:, tt:tt + 1]),
                        reads=(ps, r4[par]), writes=(vj,))

                npts = 4 if blk < 4 else 12

                def slot_point(p):
                    if p >= npts:
                        return
                    k = blk * 4 + p if blk < 4 else 16 + (blk - 4) * 12 + p
                    if k + 4 < len(TSLOTS):
                        tern_prep(k + 4)
                    tern_trans(k)

                mats_mm(0)
                slot_point(0)
                slot_point(1)
                sq_stat(0)
                mats_mm(1)
                murow_mms()
                rank1(0)
                rank1(1)
                post_a_sq(0)
                slot_point(2)
                slot_point(3)
                sq_stat(1)
                if blk + 1 < NBLK:
                    xload(blk + 1, halves=(1,))
                mats_mm(2)
                rank1(2)
                post_a_sq(1)
                post_a(0)
                slot_point(4)
                slot_point(5)
                sq_stat(2)
                mats_mm(3)
                rank1(3)
                post_a_sq(2)
                post_a(1)
                post_b(0)
                slot_point(6)
                slot_point(7)
                sq_stat(3)
                r4_chain()
                v_mm(0)
                post_a_sq(3)
                post_a(2)
                post_b(1)
                slot_point(8)
                v_mm(1)
                post_a(3)
                post_b(2)
                slot_point(9)
                v_mm(2)
                post_b(3)
                slot_point(10)
                slot_point(11)
                v_mm(3)

            xload(0)
            for kk in range(4):
                tern_prep(kk)
            for blk in range(3):
                q_block(blk)
            s_scalar_chain()
            for blk in range(3, 16):
                q_block(blk)
            gvin_keys = tuple((cc_gv_in, (g_, h_)) for g_ in range(32)
                              for h_ in range(2))
            oin_keys = tuple((cc_o_in, (i_, j8_)) for i_ in range(16)
                             for j8_ in range(2))
            kb.emit("qcc", lambda g: g.collective_compute(
                "AllGather", OP.bypass, replica_groups=[list(range(NCORES))],
                ins=[cc_o_in.ap()], outs=[cc_o_out.ap()]),
                reads=oin_keys, writes=(cc_o_out,))
            kb.emit("qcc", lambda g: g.collective_compute(
                "AllGather", OP.bypass, replica_groups=[list(range(NCORES))],
                ins=[cc_gv_in.ap()], outs=[cc_gv_out.ap()]),
                reads=gvin_keys, writes=(cc_gv_out,))

            tap("q1T", q1T[:, :], [128, BT], BF16, reads=(q1T,))
            tap("q2T", q2T[:, :], [128, BT], BF16, reads=(q2T,))
            tap("k1T", k1T[:, :], [128, BT], BF16, reads=(k1T,))
            tap("k2T", k2T[:, :], [128, BT], BF16, reads=(k2T,))
            tap("vj", vj[:, :], [128, 64 * 257], BF16, reads=(vj,))
            tap("gv_in", cc_gv_in.ap(), [DH, GV], BF16, reads=gvin_keys)
            tap("o_in", cc_o_in.ap(), [TOK, C], BF16, reads=oin_keys)

        kb.barrier()
        QB2 = 512
        with ExitStack() as sta:
            masks = [kb.sbuf(f"mask{i}", [128, 512], BF16, scope=sta)
                     for i in range(4)]
            iotat2 = kb.sbuf("iotat2", [128, 512], F32, scope=sta)
            kb.emit("pool", lambda g: g.iota(
                iotat2[:, :], pattern=[[1, 512]], base=0,
                channel_multiplier=-1,
                allow_small_or_imprecise_dtypes=True),
                writes=(iotat2,))
            for mj in range(4):
                kb.emit("dve", lambda v, mj=mj: v.tensor_single_scalar(
                    out=masks[mj][:, :], in_=iotat2[:, 0:512],
                    scalar=float(mj * 128), op=OP.is_ge),
                    reads=(iotat2,), writes=(masks[mj],))
            ystage = kb.sbuf("ystage", [128, 2 * T], BF16, scope=sta)
            e_sb = [kb.sbuf(f"e_sb{i}", [128, QB2], BF16, scope=sta)
                    for i in range(3)]
            rd_row = [kb.sbuf(f"rd_row{i}", [1, QB2], F32, scope=sta)
                      for i in range(2)]
            rd_b = [kb.sbuf(f"rd_b{i}", [128, QB2], F32, scope=sta)
                    for i in range(2)]
            z_sb = [kb.sbuf(f"z_sb{i}", [128, QB2], F32, scope=sta)
                    for i in range(2)]
            _t2buf = [kb.sbuf(f"t2buf{i}", [128, QB2], F32, scope=sta)
                      for i in range(2)]
            ps_sc = [kb.psum(f"ps_sc{i}", [128, 512], F32, scope=sta)
                     for i in range(3)]
            ps_avt = [kb.psum(f"ps_avt{i}", [128, 512], F32, scope=sta)
                      for i in range(4)]
            ps_den = kb.psum("ps_den", [128, 512], F32, scope=sta)

            for seq in range(2):
                sbase = seq * T
                for qb in range(T // QB2):
                    q0 = qb * QB2
                    n_kt = 4 * (qb + 1)
                    steps = [(a, kt) for a in range(2) for kt in range(n_kt)]

                    def emit_sc(idx, q0=q0, sbase=sbase, n_kt=n_kt, steps=steps):
                        a, kt = steps[idx]
                        qT, kT = ((q1T, k1T), (q2T, k2T))[a]
                        sc = ps_sc[idx % 3]
                        e = e_sb[idx % 3]
                        kb.emit("pe", lambda t, kT=kT, qT=qT, sc=sc, kt=kt,
                                q0=q0, sbase=sbase: t.matmul(
                            sc[:, :],
                            kT[:, sbase + kt * 128:sbase + (kt + 1) * 128],
                            qT[:, sbase + q0:sbase + q0 + QB2],
                            start=True, stop=True),
                            reads=(kT, qT), writes=(sc,))
                        kb.emit("act", lambda s, sc=sc, e=e: s.activation(
                            out=e[:, :], in_=sc[:, :], func=AF.Exp,
                            scale=ISCALE),
                            reads=(sc,), writes=(e,))
                        dj = kt - 4 * qb
                        if dj >= 0:
                            kb.emit("dve", lambda v, e=e, dj=dj: v.tensor_tensor(
                                out=e[:, :], in0=e[:, :], in1=masks[dj][:, :],
                                op=OP.mult),
                                reads=(e, masks[dj]), writes=(e,))

                    def emit_av(idx, n_kt=n_kt, steps=steps, seq=seq):
                        a, kt = steps[idx]
                        e = e_sb[idx % 3]
                        g = seq * 32 + kt
                        for half in range(2):
                            av = ps_avt[a * 2 + half]
                            kb.emit("pe", lambda t, e=e, half=half, av=av,
                                    g=g, kt=kt, n_kt=n_kt: t.matmul(
                                av[:, :],
                                vj[:, g * 257 + half * 128:g * 257 + (half + 1) * 128],
                                e[:, :],
                                start=(kt == 0), stop=(kt == n_kt - 1)),
                                reads=(e, vj), writes=(av,))
                        drow = ps_den[32 * a:32 * a + 1, :]
                        kb.emit("pe", lambda t, e=e, drow=drow, kt=kt,
                                n_kt=n_kt: t.matmul(
                            drow, ones_colb[:, 0:1], e[:, :],
                            start=(kt == 0), stop=(kt == n_kt - 1)),
                            reads=(e, ones_colb), writes=(ps_den,))
                        if kt == n_kt - 1:
                            kb.emit("dve", lambda v, a=a, drow=drow: v.reciprocal(
                                out=rd_row[a][:, :], in_=drow),
                                reads=(ps_den,), writes=(rd_row[a],))

                    emit_sc(0)
                    emit_sc(1)
                    for idx in range(len(steps)):
                        if idx + 2 < len(steps):
                            emit_sc(idx + 2)
                        emit_av(idx)
                    kb.emit("dve", lambda v: v.tensor_scalar_mul(
                        out=rd_row[1][:, :], in0=rd_row[1][:, :],
                        scalar1=neglam[0:1, 0:1]),
                        reads=(rd_row[1], neglam), writes=(rd_row[1],))
                    for a in range(2):
                        kb.emit("pe", lambda t, a=a: t.matmul(
                            ps_sc[a][:, :], onesf_row[:, :], rd_row[a][:, :],
                            start=True, stop=True),
                            reads=(onesf_row, rd_row[a]), writes=(ps_sc[a],))
                        kb.emit("act", lambda s, a=a: s.activation(
                            out=rd_b[a][:, :], in_=ps_sc[a][:, :], func=AF.Copy),
                            reads=(ps_sc[a],), writes=(rd_b[a],))
                    for half in range(2):
                        kb.emit("dve", lambda v, half=half: v.tensor_tensor(
                            out=z_sb[half][:, :], in0=ps_avt[half][:, :],
                            in1=rd_b[0][:, :], op=OP.mult),
                            reads=(ps_avt[half], rd_b[0]), writes=(z_sb[half],))
                        kb.emit("dve", lambda v, half=half: v.tensor_tensor(
                            out=_t2buf[half][:, :],
                            in0=ps_avt[2 + half][:, :], in1=rd_b[1][:, :],
                            op=OP.mult),
                            reads=(ps_avt[2 + half], rd_b[1]),
                            writes=(_t2buf[half],))
                        kb.emit("dve", lambda v, half=half, q0=q0, seq=seq: v.tensor_tensor(
                            out=ystage[:, half * T + q0:half * T + q0 + QB2],
                            in0=z_sb[half][:, :], in1=_t2buf[half][:, :],
                            op=OP.add),
                            reads=(z_sb[half], _t2buf[half]), writes=(ystage,))
                for hf in range(2):
                    kb.dma("qsp",
                           cc_y_in[seq].ap()[hf * 128:(hf + 1) * 128, :],
                           ystage[:, hf * T:(hf + 1) * T],
                           reads=(ystage,), writes=((cc_y_in[seq], hf),))
                kb.emit("qcc", lambda g, seq=seq: g.collective_compute(
                    "AllGather", OP.bypass, replica_groups=[list(range(NCORES))],
                    ins=[cc_y_in[seq].ap()],
                    outs=[cc_y_ag.ap()[seq * C:(seq + 1) * C, :]]),
                    reads=((cc_y_in[seq], 0), (cc_y_in[seq], 1)),
                    writes=((cc_y_ag, seq),))

            tap("y_in0", cc_y_in[0].ap(), [DH, T], BF16,
                reads=((cc_y_in[0], 0), (cc_y_in[0], 1)))
            tap("y_in1", cc_y_in[1].ap(), [DH, T], BF16,
                reads=((cc_y_in[1], 0), (cc_y_in[1], 1)))
        st_qa.close()
        kb.barrier()

        hqT = kb.sbuf("hqT", [128, 16 * TOK], BF16)
        with ExitStack() as stp:
            nw_sb = kb.sbuf("nw_sb", [1, C], F32, scope=stp)
            kb.dma("qsp", nw_sb[:, :], nw.ap(), reads=(nw,), writes=(nw_sb,))
            ps_o = [kb.psum(f"ps_o{i}", [128, 512], F32, scope=stp) for i in range(6)]
            ps_t2 = [kb.psum(f"ps_t2_{i}", [128, 1024], BF16, scope=stp)
                     for i in range(2)]

            nwb = kb.sbuf("nwb", [128, C], F32, scope=stp)
            for cs in range(4):
                kb.emit("pe", lambda t, cs=cs: t.matmul(
                    ps_o[0][:, :], onesf_row[:, :],
                    nw_sb[0:1, cs * 512:(cs + 1) * 512],
                    start=True, stop=True),
                    reads=(onesf_row, nw_sb), writes=(ps_o[0],))
                kb.emit("act", lambda s, cs=cs: s.activation(
                    out=nwb[:, cs * 512:(cs + 1) * 512], in_=ps_o[0][:, :],
                    func=AF.Copy),
                    reads=(ps_o[0],), writes=(nwb,))
            woT = kb.sbuf("woT", [128, 16 * C], BF16, scope=stp)
            y_loc = kb.sbuf("y_loc", [128, 16 * TOK], BF16, scope=stp)
            x1t = [kb.sbuf(f"x1t{i}", [128, C], F32, scope=stp) for i in range(2)]
            _xres1 = kb.sbuf("xres_s", [128, C], F32, scope=stp)
            ht = [kb.sbuf(f"ht{i}", [128, C], F32, scope=stp) for i in range(2)]
            hqt = [kb.sbuf(f"hqt{i}", [128, C], BF16, scope=stp) for i in range(2)]
            p2dump = kb.sbuf("p2dump", [128, C], BF16, scope=stp)
            mus4 = [kb.sbuf(f"mus4_{i}", [128, 4], F32, scope=stp)
                    for i in range(2)]
            cols = [[kb.sbuf(f"p2c{p}_{i}", [128, 1], F32, scope=stp)
                     for i in range(6)] for p in range(2)]

            cid_cell = {}
            cid_sb = kb.sbuf("cid_sb", [1, 2], mybir.dt.int32, scope=stp)
            _cid_mark = kb.emit(
                "qgp", lambda eng: eng.dma_start(out=cid_sb[:, :], in_=cid_in.ap()),
                reads=(), writes=(cid_sb,))

            def _load_cid(eng, m=_cid_mark, cell=cid_cell):
                eng.wait_ge(kb.sem[m[0]], m[1])
                cell["row"] = eng.value_load(cid_sb[0:1, 0:1])
                cell["col"] = eng.value_load(cid_sb[0:1, 1:2])
            kb.streams["pool"].append(_load_cid)
            for i in range(16):
                def _yld(eng, i=i, cell=cid_cell):
                    base = cc_y_ag.ap()[bass.ds(cell["row"], 2048),
                                        bass.ts(cell["col"], TOK)]
                    return eng.dma_start(
                        out=y_loc[:, i * TOK:(i + 1) * TOK],
                        in_=base[i * 128:(i + 1) * 128, :])
                kb.emit("qgp", _yld, reads=((cc_y_ag, 0), (cc_y_ag, 1)),
                        writes=((y_loc, i),))

            tap("y_loc", y_loc[:, :], [128, 16 * TOK], BF16,
                reads=tuple((y_loc, i_) for i_ in range(16)))

            for i in range(16):
                rdk = tuple((wo_t_dram, (j_, i // 4)) for j_ in range(16))
                kb.dma("qsp" if i % 2 == 0 else "qact",
                       woT[:, i * 2048:(i + 1) * 2048],
                       wo_t_dram.ap()[i * 128:(i + 1) * 128, :],
                       reads=rdk, writes=((woT, i),))

            def wo_mms(tt):
                par = tt % 2
                x1 = x1t[par]
                cl = cols[par]
                for cs in range(4):
                    ps = ps_o[(tt * 4 + cs) % 6]
                    for i in range(16):
                        kb.emit("pe", lambda t, tt=tt, cs=cs, i=i, ps=ps: t.matmul(
                            ps[:, :],
                            y_loc[:, i * TOK + tt * 128:i * TOK + (tt + 1) * 128],
                            woT[:, i * 2048 + cs * 512:i * 2048 + (cs + 1) * 512],
                            start=(i == 0), stop=(i == 15)),
                            reads=((y_loc, i), (woT, i)), writes=(ps,))
                    kb.emit("dve", lambda v, cs=cs, ps=ps, x1=x1,
                            par=par: v.scalar_tensor_tensor(
                        out=x1[:, cs * 512:(cs + 1) * 512], in0=ps[:, :],
                        scalar=1.0 - LAMBDA_INIT,
                        in1=_xres1[:, cs * 512:(cs + 1) * 512],
                        op0=OP.mult, op1=OP.add,
                        accum_out=mus4[par][:, cs:cs + 1]),
                        reads=(ps, (_xres1, cs)), writes=(x1, mus4[par]))
                kb.emit("act", lambda s, x1=x1, cl=cl: s.activation(
                    out=p2dump[:, :], in_=x1[:, :], func=AF.Square,
                    accum_out=cl[1][:, :]),
                    reads=(x1,), writes=(p2dump, cl[1]))

            def ln_a(tt):
                par = tt % 2
                x1 = x1t[par]
                h = ht[par]
                hq = hqt[par]
                cl = cols[par]
                kb.emit("dve", lambda v, cl=cl, par=par: v.reduce_sum(
                    out=cl[0][:, :], in_=mus4[par][:, 0:4], axis=AX.X),
                    reads=(mus4[par],), writes=(cl[0],))
                kb.emit("dve", lambda v, cl=cl: v.tensor_scalar_mul(
                    out=cl[0][:, :], in0=cl[0][:, :], scalar1=1.0 / C),
                    reads=(cl[0],), writes=(cl[0],))
                kb.emit("dve", lambda v, cl=cl: v.tensor_tensor(
                    out=cl[2][:, :], in0=cl[0][:, :], in1=cl[0][:, :],
                    op=OP.mult),
                    reads=(cl[0],), writes=(cl[2],))
                kb.emit("dve", lambda v, cl=cl: v.tensor_scalar(
                    out=cl[1][:, :], in0=cl[1][:, :], scalar1=1.0 / C,
                    scalar2=cl[2][:, 0:1], op0=OP.mult, op1=OP.subtract),
                    reads=(cl[1], cl[2]), writes=(cl[1],))
                m4 = mus4[par]
                kb.emit("dve", lambda v, cl=cl, m4=m4: v.tensor_scalar_add(
                    out=m4[:, 0:1], in0=cl[1][:, :], scalar1=LN_EPS),
                    reads=(cl[1],), writes=(m4,))
                kb.emit("dve", lambda v, cl=cl, m4=m4: v.tensor_scalar(
                    out=m4[:, 1:2], in0=cl[1][:, :], scalar1=1.0 + FFN_EPS,
                    scalar2=FFN_EPS * LN_EPS, op0=OP.mult, op1=OP.add),
                    reads=(cl[1],), writes=(m4,))
                kb.emit("dve", lambda v, m4=m4: v.reciprocal(
                    out=m4[:, 2:4], in_=m4[:, 0:2]),
                    reads=(m4,), writes=(m4,))
                kb.emit("act", lambda s, m4=m4: s.activation(
                    out=m4[:, 0:2], in_=m4[:, 2:4], func=AF.Sqrt),
                    reads=(m4,), writes=(m4,))
                kb.emit("dve", lambda v, x1=x1, h=h, cl=cl, m4=m4: v.tensor_scalar(
                    out=h[:, :], in0=x1[:, :], scalar1=cl[0][:, 0:1],
                    scalar2=m4[:, 1:2], op0=OP.subtract, op1=OP.mult),
                    reads=(x1, cl[0], m4), writes=(h,))
                kb.emit("dve", lambda v, cl=cl, m4=m4: v.tensor_scalar_add(
                    out=cl[5][:, :], in0=m4[:, 0:1], scalar1=1.0),
                    reads=(m4,), writes=(cl[5],))
                kb.emit("dve", lambda v, cl=cl, m4=m4: v.tensor_scalar(
                    out=cl[2][:, :], in0=m4[:, 0:1], scalar1=cl[0][:, 0:1],
                    scalar2=-1.0, op0=OP.mult, op1=OP.mult),
                    reads=(m4, cl[0]), writes=(cl[2],))
                kb.emit("dve", lambda v, h=h: v.tensor_tensor(
                    out=h[:, :], in0=h[:, :], in1=nwb[:, :], op=OP.mult),
                    reads=(h, nwb), writes=(h,))
                kb.emit("dve", lambda v, h=h, cl=cl: v.tensor_reduce(
                    out=cl[0][:, :], in_=h[:, :], axis=AX.X, op=OP.max,
                    apply_absolute_value=True),
                    reads=(h,), writes=(cl[0],))
                kb.emit("dve", lambda v, cl=cl: v.tensor_scalar_max(
                    out=cl[0][:, :], in0=cl[0][:, :], scalar1=1e-5),
                    reads=(cl[0],), writes=(cl[0],))
                kb.emit("dve", lambda v, tt=tt, cl=cl: v.tensor_tensor(
                    out=alph1[:, tt:tt + 1], in0=cl[0][:, 0:1],
                    in1=sgv127_col[:, 0:1], op=OP.mult),
                    reads=(cl[0], sgv127_col), writes=(alph1,))
                kb.emit("dve", lambda v, cl=cl: v.reciprocal(
                    out=cl[1][:, :], in_=cl[0][:, :]),
                    reads=(cl[0],), writes=(cl[1],))
                kb.emit("dve", lambda v, cl=cl: v.tensor_scalar_mul(
                    out=cl[1][:, :], in0=cl[1][:, :], scalar1=QB),
                    reads=(cl[1],), writes=(cl[1],))
                kb.emit("act", lambda s, h=h, cl=cl: s.activation(
                    out=h[:, :], in_=h[:, :], func=AF.Copy,
                    scale=cl[1][:, 0:1], bias=MAGIC),
                    reads=(h, cl[1]), writes=(h,))
                kb.emit("act", lambda s, h=h, hq=hq: s.activation(
                    out=hq[:, :], in_=h[:, :], func=AF.Copy, scale=1.0,
                    bias=-MAGIC),
                    reads=(h,), writes=(hq,))
                kb.emit("dve", lambda v, x1=x1, cl=cl: v.tensor_scalar(
                    out=x1[:, :], in0=x1[:, :], scalar1=cl[5][:, 0:1],
                    scalar2=cl[2][:, 0:1], op0=OP.mult, op1=OP.add),
                    reads=(x1, cl[5], cl[2]), writes=(x1,))
                kb.dma("qact", r2_dram.ap()[tt * 128:(tt + 1) * 128, :],
                       x1[:, :], reads=(x1,), writes=((r2_dram, tt),))

            def ln_b(tt):
                par = tt % 2
                hq = hqt[par]
                pst = ps_t2[par]
                for i4 in range(4):
                    for i1 in range(4):
                        i = i4 * 4 + i1
                        kb.emit("pe", lambda t, i=i, i1=i1, hq=hq, pst=pst: t.transpose(
                            pst[:, i1 * 128:(i1 + 1) * 128],
                            hq[:, i * 128:(i + 1) * 128], identb[:, :]),
                            reads=(hq, identb), writes=(pst,))
                    kb.emit("act", lambda s, i4=i4, tt=tt, pst=pst: s.activation(
                        out=hqT[:, tt * 2048 + i4 * 512:tt * 2048 + (i4 + 1) * 512],
                        in_=pst[:, 0:512], func=AF.Copy),
                        reads=(pst,), writes=(hqT,))

            for rr in range(9):
                if rr < 8:
                    for cs in range(4):
                        kb.dma("qgp", _xres1[:, cs * 512:(cs + 1) * 512],
                               x_mine.ap()[rr * 128:(rr + 1) * 128,
                                           cs * 512:(cs + 1) * 512],
                               reads=(x_mine,), writes=((_xres1, cs),))
                if 0 <= rr - 1 < 8:
                    ln_a(rr - 1)
                if rr < 8:
                    wo_mms(rr)
                if 0 <= rr - 1 < 8:
                    ln_b(rr - 1)

            tap("x1_r2", r2_dram.ap(), [TOK, C], F32,
                reads=tuple((r2_dram, t_) for t_ in range(8)))
            tap("hqT", hqT[:, :], [128, 16 * TOK], BF16, reads=(hqT,))

        kb.barrier()
        with ExitStack() as stf:
            rhs_gv = [kb.sbuf(f"rhs_gv{i}", [128, 16 * 512], BF16, scope=stf)
                      for i in range(4)]
            sg_sb = kb.sbuf("sg_sb", [128, 512], F32, scope=stf)
            h2_sb = [kb.sbuf(f"h2_sb{i}", [128, 512], F32, scope=stf)
                     for i in range(2)]
            mcol = kb.sbuf("mcol", [128, 1], F32, scope=stf)
            ps_f = [kb.psum(f"ps_f{i}", [128, 512], F32, scope=stf) for i in range(4)]

            kb.emit("dve", lambda v: v.memset(m2acc[:, :], 0.0), writes=(m2acc,))
            gv_rr = cc_gv_out.ap().rearrange("(i p) o -> i p o", p=128)
            for sp in range(16):
                bufv = rhs_gv[(sp % 2) * 2]
                bufg = rhs_gv[(sp % 2) * 2 + 1]
                for half, osl, buf in ((0, sp, bufv), (1, sp + 16, bufg)):
                    for i in range(16):
                        kb.dma("qsp", buf[:, i * 512:(i + 1) * 512],
                               gv_rr[i, :, osl * 512:(osl + 1) * 512],
                               reads=(cc_gv_out,), writes=((buf, i),))
                for tt in range(8):
                    psv = ps_f[tt % 2]
                    psg = ps_f[2 + tt % 2]
                    for i in range(16):
                        kb.emit("pe", lambda t, tt=tt, i=i, psv=psv, bufv=bufv: t.matmul(
                            psv[:, :],
                            hqT[:, tt * 2048 + i * 128:tt * 2048 + (i + 1) * 128],
                            bufv[:, i * 512:(i + 1) * 512],
                            start=(i == 0), stop=(i == 15)),
                            reads=(hqT, (bufv, i)), writes=(psv,))
                    for i in range(16):
                        kb.emit("pe", lambda t, tt=tt, i=i, psg=psg, bufg=bufg: t.matmul(
                            psg[:, :],
                            hqT[:, tt * 2048 + i * 128:tt * 2048 + (i + 1) * 128],
                            bufg[:, i * 512:(i + 1) * 512],
                            start=(i == 0), stop=(i == 15)),
                            reads=(hqT, (bufg, i)), writes=(psg,))
                    kb.emit("act", lambda s, tt=tt, psg=psg: s.activation(
                        out=sg_sb[:, :], in_=psg[:, :], func=AF.Silu,
                        scale=alph1[:, tt:tt + 1]),
                        reads=(psg, alph1), writes=(sg_sb,))
                    h2b = h2_sb[tt % 2]
                    kb.emit("dve", lambda v, tt=tt, psv=psv, h2b=h2b: v.scalar_tensor_tensor(
                        out=h2b[:, :], in0=psv[:, :], scalar=alph1[:, tt:tt + 1],
                        in1=sg_sb[:, :], op0=OP.mult, op1=OP.mult),
                        reads=(psv, alph1, sg_sb), writes=(h2b,))
                    kb.emit("dve", lambda v, h2b=h2b: v.tensor_reduce(
                        out=mcol[:, :], in_=h2b[:, :], axis=AX.X, op=OP.max,
                        apply_absolute_value=True),
                        reads=(h2b,), writes=(mcol,))
                    kb.emit("dve", lambda v, tt=tt: v.tensor_tensor(
                        out=m2acc[:, tt:tt + 1], in0=m2acc[:, tt:tt + 1],
                        in1=mcol[:, 0:1], op=OP.max),
                        reads=(m2acc, mcol), writes=(m2acc,))
                    kb.dma("qgp",
                           h2_dram.ap()[tt * 128:(tt + 1) * 128,
                                        sp * 512:(sp + 1) * 512],
                           h2b[:, :], reads=(h2b,), writes=((h2_dram, (tt, sp)),))
        kb.barrier()
        for tt in range(8):
            kb.emit("dve", lambda v, tt=tt: v.tensor_scalar_max(
                out=m2acc[:, tt:tt + 1], in0=m2acc[:, tt:tt + 1], scalar1=1e-5),
                reads=(m2acc,), writes=(m2acc,))
            kb.emit("dve", lambda v, tt=tt: v.tensor_tensor(
                out=alph2[:, tt:tt + 1], in0=m2acc[:, tt:tt + 1],
                in1=sout127_col[:, 0:1], op=OP.mult),
                reads=(m2acc, sout127_col), writes=(alph2,))
            kb.emit("dve", lambda v, tt=tt: v.reciprocal(
                out=m2s[:, tt:tt + 1], in_=m2acc[:, tt:tt + 1]),
                reads=(m2acc,), writes=(m2s,))
            kb.emit("dve", lambda v, tt=tt: v.tensor_scalar_mul(
                out=m2s[:, tt:tt + 1], in0=m2s[:, tt:tt + 1], scalar1=QB),
                reads=(m2s,), writes=(m2s,))

        hq2T = kb.sbuf("hq2T", [128, 8 * HID], BF16)
        with ExitStack() as stf2:
            h2_rb = [kb.sbuf(f"h2_rb{i}", [128, 2048], F32, scope=stf2)
                     for i in range(2)]
            h2_q = [kb.sbuf(f"h2_q{i}", [128, 2048], BF16, scope=stf2)
                    for i in range(2)]
            ps_t3 = [kb.psum(f"ps_t3{i}", [128, 1024], BF16, scope=stf2)
                     for i in range(2)]
            for tt in range(8):
                for s4 in range(4):
                    buf = h2_rb[s4 % 2]
                    hq = h2_q[s4 % 2]
                    kb.dma("qsp", buf[:, :],
                           h2_dram.ap()[tt * 128:(tt + 1) * 128,
                                        s4 * 2048:(s4 + 1) * 2048],
                           reads=tuple((h2_dram, (tt, s4 * 4 + k_)) for k_ in range(4)),
                           writes=(buf,))
                    kb.emit("dve", lambda v, buf=buf, tt=tt: v.tensor_scalar(
                        out=buf[:, :], in0=buf[:, :], scalar1=m2s[:, tt:tt + 1],
                        scalar2=MAGIC, op0=OP.mult, op1=OP.add),
                        reads=(buf, m2s), writes=(buf,))
                    kb.emit("dve", lambda v, buf=buf, hq=hq: v.tensor_scalar_add(
                        out=hq[:, :], in0=buf[:, :], scalar1=-MAGIC),
                        reads=(buf,), writes=(hq,))
                    for half in range(2):
                        pst = ps_t3[half]
                        for j in range(8):
                            i = half * 8 + j
                            kb.emit("pe", lambda t, i=i, j=j, hq=hq, pst=pst: t.transpose(
                                pst[:, j * 128:(j + 1) * 128],
                                hq[:, i * 128:(i + 1) * 128], identb[:, :]),
                                reads=(hq, identb), writes=(pst,))
                        kb.emit("dve", lambda v, tt=tt, s4=s4, half=half, pst=pst: v.tensor_copy(
                            out=hq2T[:, tt * HID + s4 * 2048 + half * 1024:
                                     tt * HID + s4 * 2048 + (half + 1) * 1024],
                            in_=pst[:, :]),
                            reads=(pst,), writes=(hq2T,))
        kb.barrier()
        with ExitStack() as stf3:
            rhs_o = [kb.sbuf(f"rhs_o{i}", [128, 2048], BF16, scope=stf3)
                     for i in range(3)]
            r2t = [kb.sbuf(f"r2t{i}", [128, 512], F32, scope=stf3) for i in range(2)]
            o_sb = [kb.sbuf(f"o_sb{i}", [128, 512], F32, scope=stf3)
                    for i in range(2)]
            ps_f3 = [kb.psum(f"ps_f3{i}", [128, 512], F32, scope=stf3)
                     for i in range(8)]
            for cs in range(4):
                for g4 in range(16):
                    buf = rhs_o[g4 % 3]
                    src = cc_o_out.ap()[g4 * 512:(g4 + 1) * 512,
                                        cs * 512:(cs + 1) * 512]
                    src = src.rearrange("(c p) n -> p c n", p=128)
                    kb.dma("qsp", buf[:, :].rearrange("p (c n) -> p c n", c=4),
                           src, reads=(cc_o_out,), writes=(buf,))
                    for tt in range(8):
                        for kc in range(4):
                            k = g4 * 4 + kc
                            kb.emit("pe", lambda t, tt=tt, kc=kc, k=k, buf=buf: t.matmul(
                                ps_f3[tt][:, :],
                                hq2T[:, tt * HID + k * 128:tt * HID + (k + 1) * 128],
                                buf[:, kc * 512:(kc + 1) * 512],
                                start=(k == 0), stop=(k == 63)),
                                reads=(hq2T, buf), writes=(ps_f3[tt],))
                for tt in range(8):
                    rb = r2t[tt % 2]
                    ob = o_sb[tt % 2]
                    kb.dma("qact", rb[:, :],
                           r2_dram.ap()[tt * 128:(tt + 1) * 128,
                                        cs * 512:(cs + 1) * 512],
                           reads=((r2_dram, tt),), writes=(rb,))
                    kb.emit("dve", lambda v, tt=tt, rb=rb, ob=ob: v.scalar_tensor_tensor(
                        out=ob[:, :], in0=ps_f3[tt][:, :], scalar=alph2[:, tt:tt + 1],
                        in1=rb[:, :], op0=OP.mult, op1=OP.add),
                        reads=(ps_f3[tt], alph2, rb), writes=(ob,))
                    kb.dma("qact",
                           out.ap()[tt * 128:(tt + 1) * 128, cs * 512:(cs + 1) * 512],
                           ob[:, :], reads=(ob,), writes=((out, (tt, cs)),))

        kb.wait_all("sp")
        kb.finalize()
    return nc


def kernel(L0_attn_x, L0_attn_wq, L0_attn_wk, L0_attn_wv, L0_attn_wo,
           L0_attn_lq1, L0_attn_lk1, L0_attn_lq2, L0_attn_lk2,
           L0_ffn_norm_w, L0_ffn_w_gv, L0_ffn_w_out):
    xf = np.ascontiguousarray(np.asarray(L0_attn_x, dtype=np.float32).reshape(BT, C))
    wqf = np.asarray(L0_attn_wq, dtype=np.float32)
    wkf = np.asarray(L0_attn_wk, dtype=np.float32)
    wvf = np.asarray(L0_attn_wv, dtype=np.float32)
    wof = np.ascontiguousarray(np.asarray(L0_attn_wo, dtype=np.float32))
    lam = np.ascontiguousarray(np.concatenate([
        np.asarray(L0_attn_lq1, dtype=np.float32),
        np.asarray(L0_attn_lk1, dtype=np.float32),
        np.asarray(L0_attn_lq2, dtype=np.float32),
        np.asarray(L0_attn_lk2, dtype=np.float32)]).reshape(1, 4 * DH))
    nwf = np.ascontiguousarray(np.asarray(L0_ffn_norm_w, dtype=np.float32).reshape(1, C))
    wgvf = np.asarray(L0_ffn_w_gv, dtype=np.float32)
    woutf = np.asarray(L0_ffn_w_out, dtype=np.float32)

    in_maps = []
    for c in range(NCORES):
        in_maps.append({
            "x": xf,
            "x_mine": np.ascontiguousarray(xf[c * TOK:(c + 1) * TOK, :]),
            "wq": np.ascontiguousarray(wqf[c * DH:(c + 1) * DH, :]),
            "wk": np.ascontiguousarray(wkf[c * DH:(c + 1) * DH, :]),
            "wv": np.ascontiguousarray(wvf[c * DH:(c + 1) * DH, :]),
            "wo": wof,
            "lam_in": lam,
            "nw": nwf,
            "wgv_sl": np.ascontiguousarray(wgvf[:, c * DH:(c + 1) * DH]),
            "wout_sl": np.ascontiguousarray(woutf[:, c * TOK:(c + 1) * TOK]),
            "cid": np.array([[(c // 4) * 2048, c % 4]], dtype=np.int32),
        })
    with allow_ici_all_to_all():
        nc = build_program()
        res = run_bass_kernel_spmd(nc, in_maps, core_ids=list(range(NCORES)))
    full = np.concatenate([np.asarray(res.results[c]["out"]) for c in range(NCORES)],
                          axis=0)
    return full.reshape(B, T, C).astype(np.float32)


# version 31
# speedup vs baseline: 1.0640x; 1.0640x over previous
import numpy as np
import ml_dtypes
import concourse.bass as bass
import concourse.mybir as mybir
from concourse.alu_op_type import AluOpType
from concourse.bass_utils import run_bass_kernel_spmd
from concourse.aot_env import allow_ici_all_to_all
from contextlib import ExitStack

F32 = mybir.dt.float32
BF16 = mybir.dt.bfloat16
AF = mybir.ActivationFunctionType
AX = mybir.AxisListType
OP = AluOpType

NCORES = 8
B, T, C = 2, 4096, 2048
BT = B * T
H, DH, HALF = 8, 256, 128
HID, GV = 8192, 16384
TOK = BT // NCORES
LAMBDA_INIT = 0.8
RMS_EPS = 1.1920929e-07
LN_EPS = 1e-05
FFN_EPS = 1e-05
MAGIC = 1.5 * 2.0 ** 23
ISCALE = 0.08838834764831845
QB = 127.0

NBLK = 16
BLK = 512
QBLK = 256
_TRUNC = None


class Kb:

    QSTEP = {"pe": 1, "dve": 1, "act": 1, "pool": 1,
             "qsp": 16, "qact": 16, "qgp": 16, "qcc": 1}
    STREAM_OF = {"pe": "pe", "dve": "dve", "act": "act", "pool": "pool",
                 "qsp": "sp", "qact": "act", "qgp": "pool", "qcc": "pool"}

    POOL = {"qsp": 10, "qgp": 6, "qact": 2, "qcc": 4}

    def __init__(self, nc, st):
        self.nc = nc
        self.st = st
        self.sem = {}
        self.cnt = {}
        for name in ("pe", "dve", "act", "pool"):
            self.sem[name] = st.enter_context(nc.semaphore(name))
            self.cnt[name] = 0
        for q, k in self.POOL.items():
            for i in range(k):
                nm = f"{q}{i}"
                self.sem[nm] = st.enter_context(nc.semaphore(nm))
                self.cnt[nm] = 0
        self.rot = {q: 0 for q in self.POOL}
        self.streams = {"pe": [], "dve": [], "act": [], "pool": [], "sp": []}
        self.waited = {s: {} for s in self.streams}
        self.lastw = {}
        self.readers = {}

    def sbuf(self, name, shape, dtype=F32, scope=None):
        h = (scope or self.st).enter_context(self.nc.sbuf_tensor(name, shape, dtype))
        return h

    def psum(self, name, shape, dtype=F32, scope=None):
        h = (scope or self.st).enter_context(self.nc.psum_tensor(name, shape, dtype))
        return h

    def dram(self, name, shape, dtype=F32, kind="Internal"):
        h = self.nc.dram_tensor(name, shape, dtype, kind=kind)
        return h

    @staticmethod
    def _key(b):
        if isinstance(b, tuple):
            return b[0].name + "#" + str(b[1])
        return b.name

    def _emit_on(self, qname, sname, step, fn, reads, writes, extra_gate=None):
        stream = self.STREAM_OF[qname]
        need = {}

        def add(sem, cnt):
            if need.get(sem, -1) < cnt:
                need[sem] = cnt
        rkeys = [self._key(b) for b in reads]
        wkeys = [self._key(b) for b in writes]
        for k in rkeys:
            for sem, cnt in self.lastw.get(k, {}).items():
                add(sem, cnt)
        for k in wkeys:
            for sem, cnt in self.lastw.get(k, {}).items():
                add(sem, cnt)
            for sem, cnt in self.readers.get(k, {}).items():
                add(sem, cnt)
        if stream == "pe":
            need.pop("pe", None)
        if extra_gate:
            add(*extra_gate)
        waits = []
        for sem, cnt in sorted(need.items()):
            if self.waited[stream].get(sem, -1) < cnt:
                waits.append((sem, cnt))
                self.waited[stream][sem] = cnt
        self.cnt[sname] += step
        mark = (sname, self.cnt[sname])
        for k in wkeys:
            lw = self.lastw.setdefault(k, {})
            lw[sname] = mark[1]
            self.readers[k] = {}
        for k in rkeys:
            rd = self.readers.setdefault(k, {})
            if rd.get(sname, -1) < mark[1]:
                rd[sname] = mark[1]
        sem_h = self.sem[sname]
        semmap = self.sem

        def closure(eng, waits=tuple(waits), fn=fn, sem_h=sem_h, step=step):
            for s, c in waits:
                eng.wait_ge(semmap[s], c)
            fn(eng).then_inc(sem_h, step)
        self.streams[stream].append(closure)
        return mark

    def emit(self, qname, fn, reads=(), writes=()):
        if qname in self.POOL:
            k = self.POOL[qname]
            i = self.rot[qname]
            self.rot[qname] = (i + 1) % k
            sname = f"{qname}{i}"
            prev = self.cnt[sname]
            gate = (sname, prev) if prev else None
            step = self.QSTEP[qname]
            return self._emit_on(qname, sname, step, fn, reads, writes,
                                 extra_gate=gate)
        return self._emit_on(qname, qname, self.QSTEP[qname], fn, reads, writes)

    def dma(self, qname, out_ap, in_ap, reads=(), writes=()):
        self.emit(qname, lambda eng, o=out_ap, i=in_ap: eng.dma_start(out=o, in_=i),
                  reads=reads, writes=writes)

    def wait_all(self, stream, exclude=()):
        for s in self.cnt:
            if any(s.startswith(e) for e in exclude):
                continue
            c = self.cnt[s]
            if c and self.waited[stream].get(s, -1) < c:
                self.waited[stream][s] = c
                self.streams[stream].append(
                    lambda eng, s=s, c=c: eng.wait_ge(self.sem[s], c))

    def barrier(self):
        for stream in self.streams:
            self.wait_all(stream, exclude=("qcc",))

    def finalize(self):
        if _TRUNC is not None:
            for k in self.streams:
                f = _TRUNC.get(k, 1.0) if isinstance(_TRUNC, dict) else _TRUNC
                n = int(len(self.streams[k]) * f) if f <= 1.0 else int(f)
                self.streams[k] = self.streams[k][:n]
        nc = self.nc
        with nc.Block() as block:
            @block.tensor
            def _(t):
                for f in self.streams["pe"]:
                    f(t)

            @block.vector
            def _(v):
                for f in self.streams["dve"]:
                    f(v)

            @block.scalar
            def _(s):
                for f in self.streams["act"]:
                    f(s)

            @block.gpsimd
            def _(g):
                for f in self.streams["pool"]:
                    f(g)

            @block.sync
            def _(sy):
                for f in self.streams["sp"]:
                    f(sy)


def brow(ap, parts=128):
    return ap.partition_broadcast(parts)


def build_program(debug_taps=()):
    taps = list(debug_taps)
    nc = bass.Bass("TRN2", target_bir_lowering=False, debug=False)

    def ext_in(name, shape, dtype=F32):
        h = nc.dram_tensor(name, shape, dtype, kind="ExternalInput")
        return h

    x = ext_in("x", [BT, C])
    cid_in = nc.dram_tensor("cid", [1, 2], mybir.dt.int32, kind="ExternalInput")
    x_mine = ext_in("x_mine", [TOK, C])
    wq = ext_in("wq", [DH, C])
    wk = ext_in("wk", [DH, C])
    wv = ext_in("wv", [DH, C])
    wo = ext_in("wo", [C, C])
    lam_in = ext_in("lam_in", [1, 4 * DH])
    nw = ext_in("nw", [1, C])
    wgv_sl = ext_in("wgv_sl", [GV, DH])
    wout_sl = ext_in("wout_sl", [C, TOK])
    out = nc.dram_tensor("out", [TOK, C], F32, kind="ExternalOutput")

    with ExitStack() as st:
        kb = Kb(nc, st)

        cc_y_in = [kb.dram(f"cc_y_in{s}", [DH, T], BF16) for s in range(2)]
        cc_y_ag = nc.dram_tensor("cc_y_ag", [2 * C, T], BF16, addr_space="Shared")
        cc_gv_in = kb.dram("cc_gv_in", [DH, GV], BF16)
        cc_gv_out = nc.dram_tensor("cc_gv_out", [C, GV], BF16, addr_space="Shared")
        cc_o_in = kb.dram("cc_o_in", [TOK, C], BF16)
        wo_t_dram = kb.dram("wo_t", [C, C], BF16)
        cc_o_out = nc.dram_tensor("cc_o_out", [HID, C], BF16, addr_space="Shared")
        cc_s_in = kb.dram("cc_s_in", [1, 2], F32)
        cc_s_out = kb.dram("cc_s_out", [1, 2], F32)
        h2_dram = kb.dram("h2_dram", [TOK, HID], F32)
        r2_dram = kb.dram("r2_dram", [TOK, C], F32)

        def tap(name, src_ap, shape, dtype=F32, reads=()):
            if name not in taps:
                return
            t = nc.dram_tensor("tap_" + name, shape, dtype, kind="ExternalOutput")
            kb.dma("qsp", t.ap(), src_ap, reads=reads, writes=(t,))

        identf = kb.sbuf("identf", [128, 128], F32)
        identb = kb.sbuf("identb", [128, 128], BF16)
        ones_colb = kb.sbuf("ones_colb", [128, 1], BF16)
        onesf_row = kb.sbuf("onesf_row", [1, 128], F32)
        neglam = kb.sbuf("neglam", [1, 1], F32)
        s_pair = kb.sbuf("s_pair", [1, 4], F32)
        invs = kb.sbuf("invs", [1, 2], F32)
        sgv127_col = kb.sbuf("sgv127_col", [128, 1], F32)
        sout127_col = kb.sbuf("sout127_col", [128, 1], F32)
        invs_gv_col = kb.sbuf("invs_gv_col", [128, 1], F32)
        invs_o_col = kb.sbuf("invs_o_col", [128, 1], F32)
        alph1 = kb.sbuf("alph1", [128, 8], F32)
        alph2 = kb.sbuf("alph2", [128, 8], F32)
        m2acc = kb.sbuf("m2acc", [128, 8], F32)
        m2s = kb.sbuf("m2s", [128, 8], F32)

        st_iota = st.enter_context(ExitStack())
        iotat = kb.sbuf("iotat", [128, 512], F32, scope=st_iota)
        lam_sb = kb.sbuf("lam_sb", [1, 4 * DH], F32, scope=st_iota)
        lamv = kb.sbuf("lamv", [1, 2 * DH], F32, scope=st_iota)
        lam_red = kb.sbuf("lam_red", [1, 4], F32, scope=st_iota)
        kb.emit("pool", lambda g: g.iota(iotat[:, :], pattern=[[1, 512]], base=0,
                                         channel_multiplier=-1,
                                         allow_small_or_imprecise_dtypes=True),
                writes=(iotat,))
        kb.emit("dve", lambda v: v.tensor_single_scalar(
            out=identf[:, :], in_=iotat[:, 0:128], scalar=0.0, op=OP.is_equal),
            reads=(iotat,), writes=(identf,))
        kb.emit("dve", lambda v: v.tensor_copy(out=identb[:, :], in_=identf[:, :]),
                reads=(identf,), writes=(identb,))
        kb.emit("dve", lambda v: v.memset(ones_colb[:, :], 1.0), writes=(ones_colb,))
        kb.emit("dve", lambda v: v.memset(onesf_row[:, :], 1.0), writes=(onesf_row,))
        kb.dma("qsp", lam_sb[:, :], lam_in.ap(), reads=(lam_in,), writes=(lam_sb,))
        kb.emit("dve", lambda v: v.tensor_tensor(
            out=lamv[0:1, 0:DH], in0=lam_sb[0:1, 0:DH],
            in1=lam_sb[0:1, DH:2 * DH], op=OP.mult),
            reads=(lam_sb,), writes=(lamv,))
        kb.emit("dve", lambda v: v.tensor_tensor(
            out=lamv[0:1, DH:2 * DH], in0=lam_sb[0:1, 2 * DH:3 * DH],
            in1=lam_sb[0:1, 3 * DH:4 * DH], op=OP.mult),
            reads=(lam_sb,), writes=(lamv,))
        kb.emit("dve", lambda v: v.reduce_sum(out=lam_red[0:1, 0:1],
                                              in_=lamv[0:1, 0:DH], axis=AX.X),
                reads=(lamv,), writes=(lam_red,))
        kb.emit("dve", lambda v: v.reduce_sum(out=lam_red[0:1, 1:2],
                                              in_=lamv[0:1, DH:2 * DH], axis=AX.X),
                reads=(lamv,), writes=(lam_red,))
        kb.emit("act", lambda s: s.activation(out=lam_red[0:1, 2:4],
                                              in_=lam_red[0:1, 0:2], func=AF.Exp),
                reads=(lam_red,), writes=(lam_red,))
        kb.emit("dve", lambda v: v.tensor_scalar(
            out=neglam[:, :], in0=lam_red[0:1, 3:4], scalar1=lam_red[0:1, 2:3],
            scalar2=-LAMBDA_INIT, op0=OP.subtract, op1=OP.add),
            reads=(lam_red,), writes=(neglam,))
        st_iota.close()

        wout_r = wout_sl.ap().rearrange("(g p) n -> g p n", p=128)

        def wgv_src(g):
            return wgv_sl.ap()[g * 1024:(g + 1) * 1024, :].rearrange(
                "(c p) n -> p c n", p=128)

        def x_src(blk, half):
            r0 = blk * BLK + half * 256
            return x.ap()[r0:r0 + 256, :].rearrange("(c p) n -> p c n", p=128)

        st_qa = st.enter_context(ExitStack())
        q1T = kb.sbuf("q1T", [128, BT], BF16, scope=st_qa)
        q2T = kb.sbuf("q2T", [128, BT], BF16, scope=st_qa)
        k1T = kb.sbuf("k1T", [128, BT], BF16, scope=st_qa)
        k2T = kb.sbuf("k2T", [128, BT], BF16, scope=st_qa)
        vj = kb.sbuf("vj", [128, 64 * 257], BF16, scope=st_qa)
        kb.emit("dve", lambda v: v.memset(vj[:, :], 1.0), writes=(vj,))

        wqT = kb.sbuf("wqT", [128, 16 * DH], BF16, scope=st_qa)
        wkT = kb.sbuf("wkT", [128, 16 * DH], BF16, scope=st_qa)
        wvT = kb.sbuf("wvT", [128, 16 * DH], BF16, scope=st_qa)
        negrs = kb.sbuf("negrs", [1, 3 * DH], BF16, scope=st_qa)

        with ExitStack() as st0:
            wstage = [kb.sbuf(f"wstage{i}", [128, C], F32, scope=st0) for i in range(2)]
            wstage_b = [kb.sbuf(f"wstage_b{i}", [128, C], BF16, scope=st0)
                        for i in range(2)]
            wrs_col = [kb.sbuf(f"wrs_col{i}", [128, 1], F32, scope=st0)
                       for i in range(2)]
            ps_rows = kb.psum("ps_rows", [128, 512], F32, scope=st0)
            ps_tw0 = [kb.psum(f"ps_tw0_{i}", [128, 1024], BF16, scope=st0)
                      for i in range(2)]
            wg_a = [kb.sbuf(f"wg_a{i}", [128, 2048], F32, scope=st0)
                    for i in range(4)]
            acc2 = kb.sbuf("acc2", [128, 2], F32, scope=st0)
            tmp_col = kb.sbuf("tmp_col", [128, 1], F32, scope=st0)
            kb.emit("dve", lambda v: v.memset(acc2[:, :], 0.0), writes=(acc2,))

            wpa_jobs = [(srcsel, g) for g in range(16) for srcsel in (0, 1)]

            def wpa_group(idx):
                srcsel, g = wpa_jobs[idx]
                col = srcsel
                w = 2048 if srcsel == 0 else 1024
                buf = wg_a[idx % 4]
                if srcsel == 0:
                    kb.dma("qsp", buf[:, 0:w].rearrange("p (c n) -> p c n", c=8),
                           wgv_src(g), reads=(wgv_sl,), writes=(buf,))
                else:
                    kb.dma("qsp", buf[:, 0:w], wout_r[g, :, :],
                           reads=(wout_sl,), writes=(buf,))
                kb.emit("dve", lambda v, w=w, buf=buf: v.reduce_sum(
                    out=tmp_col[:, :], in_=buf[:, 0:w], axis=AX.X,
                    apply_absolute_value=True),
                    reads=(buf,), writes=(tmp_col,))
                kb.emit("dve", lambda v, col=col: v.tensor_tensor(
                    out=acc2[:, col:col + 1], in0=acc2[:, col:col + 1],
                    in1=tmp_col[:, 0:1], op=OP.add),
                    reads=(acc2, tmp_col), writes=(acc2,))

            wpa_i = [0]

            def emit_wpa(n):
                for _ in range(n):
                    if wpa_i[0] < len(wpa_jobs):
                        wpa_group(wpa_i[0])
                        wpa_i[0] += 1

            stage = 0
            for wi, (wsrc, wdst) in enumerate(((wq, wqT), (wk, wkT), (wv, wvT))):
                for hf in range(2):
                    sp = stage % 2
                    kb.dma("qsp", wstage[sp][:, :],
                           wsrc.ap()[hf * 128:(hf + 1) * 128, :],
                           reads=(wsrc,), writes=(wstage[sp],))
                    kb.emit("dve", lambda v, sp=sp: v.reduce_sum(
                        out=wrs_col[sp][:, :], in_=wstage[sp][:, :], axis=AX.X),
                        reads=(wstage[sp],), writes=(wrs_col[sp],))
                    kb.emit("pe", lambda t, sp=sp: t.matmul(
                        ps_rows[0:1, 0:128], wrs_col[sp][:, 0:1], identf[:, :],
                        start=True, stop=True),
                        reads=(wrs_col[sp], identf), writes=(ps_rows,))
                    dst0 = wi * DH + hf * 128
                    kb.emit("act", lambda s, dst0=dst0: s.activation(
                        out=negrs[:, dst0:dst0 + 128], in_=ps_rows[0:1, 0:128],
                        func=AF.Copy, scale=-1.0),
                        reads=(ps_rows,), writes=(negrs,))
                    kb.emit("dve", lambda v, sp=sp: v.tensor_copy(
                        out=wstage_b[sp][:, :], in_=wstage[sp][:, :]),
                        reads=(wstage[sp],), writes=(wstage_b[sp],))
                    for i8 in range(2):
                        ptw = ps_tw0[i8]
                        for j in range(8):
                            i = i8 * 8 + j
                            kb.emit("pe", lambda t, i=i, j=j, sp=sp, ptw=ptw: t.transpose(
                                ptw[:, j * 128:(j + 1) * 128],
                                wstage_b[sp][:, i * 128:(i + 1) * 128], identb[:, :]),
                                reads=(wstage_b[sp], identb), writes=(ptw,))
                        if wi < 2:
                            d0 = hf * 2048 + i8 * 8 * 128
                            kb.emit("dve", lambda v, d0=d0, wdst=wdst, ptw=ptw: v.tensor_copy(
                                out=wdst[:, d0:d0 + 1024], in_=ptw[:, :]),
                                reads=(ptw,), writes=(wdst,))
                        else:
                            for j in range(8):
                                i = i8 * 8 + j
                                d0 = i * 256 + hf * 128
                                kb.emit("dve", lambda v, d0=d0, j=j, ptw=ptw: v.tensor_copy(
                                    out=wvT[:, d0:d0 + 128],
                                    in_=ptw[:, j * 128:(j + 1) * 128]),
                                    reads=(ptw,), writes=(wvT,))
                    emit_wpa(6)
                    stage += 1
            emit_wpa(len(wpa_jobs))
            for col in range(2):
                kb.emit("pe", lambda t, col=col: t.matmul(
                    ps_rows[0:1, 0:128], acc2[:, col:col + 1], identf[:, :],
                    start=True, stop=True),
                    reads=(acc2, identf), writes=(ps_rows,))
                kb.emit("dve", lambda v: v.tensor_copy(out=wg_a[0][0:1, 0:128],
                                                       in_=ps_rows[0:1, 0:128]),
                        reads=(ps_rows,), writes=(wg_a[0],))
                kb.emit("dve", lambda v, col=col: v.reduce_sum(
                    out=s_pair[0:1, col:col + 1], in_=wg_a[0][0:1, 0:128], axis=AX.X),
                    reads=(wg_a[0],), writes=(s_pair,))
            kb.dma("qgp", cc_s_in.ap(), s_pair[0:1, 0:2], reads=(s_pair,),
                   writes=(cc_s_in,))
            kb.emit("qcc", lambda g: g.collective_compute(
                "AllReduce", OP.add, replica_groups=[list(range(NCORES))],
                ins=[cc_s_in.ap()], outs=[cc_s_out.ap()]),
                reads=(cc_s_in,), writes=(cc_s_out,))

        kb.barrier()
        with ExitStack() as stq:
            xbf = [kb.sbuf(f"xbf{i}", [128, 2 * C], BF16, scope=stq) for i in range(3)]
            xTb = kb.sbuf("xTb", [128, 16 * BLK], BF16, scope=stq)
            sqs = kb.sbuf("sqs", [128, C], BF16, scope=stq)
            qsq = [kb.sbuf(f"qsq{i}", [128, BLK], BF16, scope=stq) for i in range(2)]
            mu4 = [kb.sbuf(f"mu4_{i}", [128, 4], F32, scope=stq) for i in range(2)]
            mq4 = [kb.sbuf(f"mq4_{i}", [128, 4], F32, scope=stq) for i in range(2)]
            mu4b = [kb.sbuf(f"mu4b_{i}", [128, 4], BF16, scope=stq) for i in range(2)]
            r4 = [kb.sbuf(f"r4_{i}", [128, 4], F32, scope=stq) for i in range(2)]
            scol = [kb.sbuf(f"scol{i}", [128, 1], F32, scope=stq) for i in range(4)]
            murow = [kb.sbuf(f"murow{i}", [1, BLK], BF16, scope=stq) for i in range(2)]
            rrow = [kb.sbuf(f"rrow{i}", [1, BLK], F32, scope=stq) for i in range(2)]
            rrow2 = [kb.sbuf(f"rrow2_{i}", [1, BLK], F32, scope=stq) for i in range(2)]
            rrb = [kb.sbuf(f"rrb{i}", [128, BLK], F32, scope=stq) for i in range(2)]
            wg_ldh = [kb.sbuf(f"wg_ldh{r}", [128, 512], F32, scope=stq)
                      for r in range(5)]
            wg_tnh = [kb.sbuf(f"wg_tnh{r}", [128, 512], BF16, scope=stq)
                      for r in range(5)]
            tern_sth = [kb.sbuf(f"tern_sth{r}", [128, 512], BF16, scope=stq)
                        for r in range(2)]

            ps_q = [kb.psum(f"ps_q{i}", [128, 512], F32, scope=stq) for i in range(4)]
            ps_misc = [kb.psum(f"ps_misc{i}", [128, 512], F32, scope=stq)
                       for i in range(2)]
            ps_xt = [kb.psum(f"ps_xt{i}", [128, 1024], BF16, scope=stq)
                     for i in range(2)]

            def s_scalar_chain():
                kb.dma("qsp", s_pair[0:1, 2:4], cc_s_out.ap(),
                       reads=(cc_s_out,), writes=(s_pair,))
                for j, ntot in ((0, GV * C), (1, C * HID)):
                    kb.emit("dve", lambda v, j=j, nt=ntot: v.tensor_scalar(
                        out=s_pair[0:1, j:j + 1], in0=s_pair[0:1, 2 + j:3 + j],
                        scalar1=1.0 / nt, scalar2=1e-5, op0=OP.mult, op1=OP.max),
                        reads=(s_pair,), writes=(s_pair,))
                    kb.emit("dve", lambda v, j=j: v.reciprocal(
                        out=invs[0:1, j:j + 1], in_=s_pair[0:1, j:j + 1]),
                        reads=(s_pair,), writes=(invs,))
                specs = [(invs_gv_col, invs, 0, 1.0), (invs_o_col, invs, 1, 1.0),
                         (sgv127_col, s_pair, 0, 1.0 / QB),
                         (sout127_col, s_pair, 1, 1.0 / QB)]
                for dst, srcb, colj, scl in specs:
                    kb.emit("pe", lambda t, srcb=srcb, colj=colj: t.matmul(
                        ps_misc[1][:, 0:1], onesf_row[:, :], srcb[0:1, colj:colj + 1],
                        start=True, stop=True),
                        reads=(onesf_row, srcb), writes=(ps_misc[1],))
                    kb.emit("act", lambda s, dst=dst, scl=scl: s.activation(
                        out=dst[:, :], in_=ps_misc[1][:, 0:1], func=AF.Copy, scale=scl),
                        reads=(ps_misc[1],), writes=(dst,))

            _wo_slots = [("wo", j, q) for j in range(16) for q in range(4)]
            _gvo = ([("gv", gh, hf) for gh in range(32) for hf in range(2)]
                    + [("o", i, j8) for i in range(16) for j8 in range(2)])
            TSLOTS = list(_wo_slots[:16])
            _wi, _gi = 16, 0
            for _b in range(12):
                for _idx in range(8):
                    TSLOTS.append(_gvo[_gi])
                    _gi += 1
                    if _idx % 2 == 1 and _wi < 64:
                        TSLOTS.append(_wo_slots[_wi])
                        _wi += 1
            assert len(TSLOTS) == 160 and _wi == 64 and _gi == 96

            def tern_prep(k):
                kind, a, b2 = TSLOTS[k]
                r = k % 5
                if kind == "wo":
                    kb.dma("qsp", wg_ldh[r][:, :],
                           wo.ap()[a * 128:(a + 1) * 128,
                                   b2 * 512:(b2 + 1) * 512],
                           reads=(wo,), writes=(wg_ldh[r],))
                    kb.emit("dve", lambda v, r=r: v.tensor_copy(
                        out=wg_tnh[r][:, :], in_=wg_ldh[r][:, :]),
                        reads=(wg_ldh[r],), writes=(wg_tnh[r],))
                    return
                if kind == "gv":
                    src = wgv_sl.ap()[a * 512:(a + 1) * 512,
                                      b2 * 128:(b2 + 1) * 128].rearrange(
                        "(c p) n -> p c n", p=128)
                    dst = wg_ldh[r][:, :].rearrange("p (c n) -> p c n", c=4)
                    rd, scol_ = wgv_sl, invs_gv_col
                else:
                    src = wout_r[a, :, b2 * 512:(b2 + 1) * 512]
                    dst = wg_ldh[r][:, :]
                    rd, scol_ = wout_sl, invs_o_col
                kb.dma("qsp", dst, src, reads=(rd,), writes=(wg_ldh[r],))
                kb.emit("act", lambda s, r=r, scol_=scol_: s.activation(
                    out=wg_ldh[r][:, :], in_=wg_ldh[r][:, :], func=AF.Copy,
                    scale=scol_[:, 0:1], bias=MAGIC),
                    reads=(wg_ldh[r], scol_), writes=(wg_ldh[r],))
                kb.emit("dve", lambda v, r=r: v.tensor_scalar(
                    out=wg_ldh[r][:, :], in0=wg_ldh[r][:, :], scalar1=-MAGIC,
                    scalar2=1.0, op0=OP.add, op1=OP.min),
                    reads=(wg_ldh[r],), writes=(wg_ldh[r],))
                kb.emit("dve", lambda v, r=r: v.tensor_scalar_max(
                    out=wg_tnh[r][:, :], in0=wg_ldh[r][:, :], scalar1=-1.0),
                    reads=(wg_ldh[r],), writes=(wg_tnh[r],))

            def tern_trans(k):
                kind, a, b2 = TSLOTS[k]
                r = k % 5
                pr = k % 2
                pxs = ps_xt[k % 2]
                for cseg in range(4):
                    kb.emit("pe", lambda t, cseg=cseg, r=r, pxs=pxs: t.transpose(
                        pxs[:, cseg * 128:(cseg + 1) * 128],
                        wg_tnh[r][:, cseg * 128:(cseg + 1) * 128],
                        identb[:, :]),
                        reads=(wg_tnh[r], identb), writes=(pxs,))
                kb.emit("act", lambda s, pr=pr, pxs=pxs: s.activation(
                    out=tern_sth[pr][:, :], in_=pxs[:, 0:512],
                    func=AF.Copy),
                    reads=(pxs,), writes=(tern_sth[pr],))
                if kind == "wo":
                    dstw = wo_t_dram.ap()[b2 * 512:(b2 + 1) * 512,
                                          a * 128:(a + 1) * 128].rearrange(
                        "(c p) n -> p c n", p=128)
                    kb.dma("qact", dstw,
                           tern_sth[pr][:, :].rearrange("p (c n) -> p c n", c=4),
                           reads=(tern_sth[pr],),
                           writes=((wo_t_dram, (a, b2)),))
                elif kind == "gv":
                    kb.dma("qact",
                           cc_gv_in.ap()[b2 * 128:(b2 + 1) * 128,
                                         a * 512:(a + 1) * 512],
                           tern_sth[pr][:, :],
                           reads=(tern_sth[pr],), writes=((cc_gv_in, (a, b2)),))
                else:
                    dsto = cc_o_in.ap()[b2 * 512:(b2 + 1) * 512,
                                        a * 128:(a + 1) * 128].rearrange(
                        "(c p) n -> p c n", p=128)
                    kb.dma("qact", dsto,
                           tern_sth[pr][:, :].rearrange("p (c n) -> p c n", c=4),
                           reads=(tern_sth[pr],), writes=((cc_o_in, (a, b2)),))

            def xload(blk, halves=(0, 1)):
                for half in halves:
                    xb = xbf[(blk * 2 + half) % 3]
                    kb.dma("qgp", xb[:, :].rearrange("p (c n) -> p c n", c=2),
                           x_src(blk, half), reads=(x,), writes=(xb,))

            def q_block(blk):
                base = blk * BLK
                par = blk % 2
                if blk + 1 < NBLK:
                    xload(blk + 1, halves=(0,))

                def xb_at(seg):
                    return xbf[(blk * 2 + seg // 2) % 3], (seg % 2) * C
                for i2 in range(8):
                    pxt = ps_xt[i2 % 2]
                    for j in range(2):
                        i = i2 * 2 + j
                        for seg in range(4):
                            xb, c0 = xb_at(seg)
                            kb.emit("pe", lambda t, i=i, j=j, seg=seg, xb=xb,
                                    c0=c0, pxt=pxt: t.transpose(
                                pxt[:, j * 512 + seg * 128:j * 512 + (seg + 1) * 128],
                                xb[:, c0 + i * 128:c0 + (i + 1) * 128],
                                identb[:, :]),
                                reads=(xb, identb), writes=(pxt,))
                    if i2 % 2 == 0:
                        kb.emit("dve", lambda v, i2=i2, pxt=pxt: v.tensor_copy(
                            out=xTb[:, i2 * 1024:(i2 + 1) * 1024], in_=pxt[:, :]),
                            reads=(pxt,), writes=(xTb,))
                    else:
                        kb.emit("act", lambda s, i2=i2, pxt=pxt: s.activation(
                            out=xTb[:, i2 * 1024:(i2 + 1) * 1024], in_=pxt[:, :],
                            func=AF.Copy),
                            reads=(pxt,), writes=(xTb,))
                for seg in range(4):
                    xb, c0 = xb_at(seg)
                    if seg < 2:
                        kb.emit("dve", lambda v, xb=xb, c0=c0, seg=seg,
                                par=par: v.reduce_sum(
                            out=mu4[par][:, seg:seg + 1], in_=xb[:, c0:c0 + C],
                            axis=AX.X),
                            reads=(xb,), writes=(mu4[par],))
                    else:
                        kb.emit("act", lambda s, xb=xb, c0=c0, seg=seg,
                                par=par: s.activation(
                            out=sqs[:, :], in_=xb[:, c0:c0 + C], func=AF.Copy,
                            accum_out=mu4[par][:, seg:seg + 1]),
                            reads=(xb,), writes=(sqs, mu4[par]))
                kb.emit("dve", lambda v, par=par: v.tensor_scalar_mul(
                    out=mu4[par][:, :], in0=mu4[par][:, :], scalar1=1.0 / C),
                    reads=(mu4[par],), writes=(mu4[par],))
                kb.emit("dve", lambda v, par=par: v.tensor_copy(
                    out=mu4b[par][:, :], in_=mu4[par][:, :]),
                    reads=(mu4[par],), writes=(mu4b[par],))
                kb.emit("dve", lambda v, par=par: v.tensor_tensor(
                    out=mq4[par][:, :], in0=mu4[par][:, :], in1=mu4[par][:, :],
                    op=OP.mult),
                    reads=(mu4[par],), writes=(mq4[par],))

                def sq_stat(seg):
                    xb, c0 = xb_at(seg)
                    kb.emit("act", lambda s, xb=xb, c0=c0, seg=seg,
                            par=par: s.activation(
                        out=sqs[:, :], in_=xb[:, c0:c0 + C],
                        func=AF.Square, accum_out=r4[par][:, seg:seg + 1]),
                        reads=(xb,), writes=(sqs, r4[par]))

                def r4_chain():
                    kb.emit("dve", lambda v, par=par: v.scalar_tensor_tensor(
                        out=r4[par][:, :], in0=r4[par][:, :], scalar=1.0 / C,
                        in1=mq4[par][:, :], op0=OP.mult, op1=OP.subtract),
                        reads=(r4[par], mq4[par]), writes=(r4[par],))
                    kb.emit("dve", lambda v, par=par: v.tensor_scalar_add(
                        out=r4[par][:, :], in0=r4[par][:, :], scalar1=LN_EPS),
                        reads=(r4[par],), writes=(r4[par],))
                    kb.emit("dve", lambda v, par=par: v.reciprocal(
                        out=mq4[par][:, :], in_=r4[par][:, :]),
                        reads=(r4[par],), writes=(mq4[par],))
                    kb.emit("act", lambda s, par=par: s.activation(
                        out=r4[par][:, :], in_=mq4[par][:, :], func=AF.Sqrt),
                        reads=(mq4[par],), writes=(r4[par],))
                def murow_mms():
                    for seg in range(4):
                        kb.emit("pe", lambda t, seg=seg, par=par: t.matmul(
                            ps_misc[0][0:1, seg * 128:(seg + 1) * 128],
                            mu4b[par][:, seg:seg + 1], identb[:, :],
                            start=True, stop=True),
                            reads=(mu4b[par], identb), writes=(ps_misc[0],))
                    kb.emit("dve", lambda v, par=par: v.tensor_copy(
                        out=murow[par][:, :], in_=ps_misc[0][0:1, 0:512]),
                        reads=(ps_misc[0],), writes=(murow[par],))

                mats = [(0, 0, ps_q[0], q1T), (0, 1, ps_q[1], q2T),
                        (1, 0, ps_q[2], k1T), (1, 1, ps_q[3], k2T)]

                def mats_mm(m):
                    wi, hf, ps, dstT = mats[m]
                    wT = wqT if wi == 0 else wkT
                    for i in range(16):
                        kb.emit("pe", lambda t, wT=wT, hf=hf, i=i, ps=ps: t.matmul(
                            ps[:, :],
                            wT[:, hf * 2048 + i * 128:hf * 2048 + (i + 1) * 128],
                            xTb[:, i * BLK:(i + 1) * BLK],
                            start=(i == 0), stop=False),
                            reads=(wT, xTb), writes=(ps,))

                def rank1(m):
                    wi, hf, ps, dstT = mats[m]
                    rs0 = wi * DH + hf * 128
                    kb.emit("pe", lambda t, rs0=rs0, ps=ps, par=par: t.matmul(
                        ps[:, :], negrs[0:1, rs0:rs0 + 128], murow[par][:, :],
                        start=False, stop=True),
                        reads=(negrs, murow[par]), writes=(ps,))

                def post_a(m):
                    _, _, ps, _ = mats[m]
                    mp = m % 2
                    kb.emit("act", lambda s, ps=ps, mp=mp: s.activation(
                        out=qsq[mp][:, :], in_=ps[:, :], func=AF.Square),
                        reads=(ps,), writes=(qsq[mp],))
                    kb.emit("pe", lambda t, mp=mp: t.matmul(
                        ps_misc[1][32:33, 0:512], ones_colb[:, 0:1], qsq[mp][:, :],
                        start=True, stop=True),
                        reads=(ones_colb, qsq[mp]), writes=(ps_misc[1],))
                    kb.emit("dve", lambda v, mp=mp: v.tensor_scalar(
                        out=rrow[mp][:, :], in0=ps_misc[1][32:33, 0:512],
                        scalar1=1.0 / HALF, scalar2=RMS_EPS,
                        op0=OP.mult, op1=OP.add),
                        reads=(ps_misc[1],), writes=(rrow[mp],))
                    kb.emit("dve", lambda v, mp=mp: v.reciprocal(
                        out=rrow2[mp][:, :], in_=rrow[mp][:, :]),
                        reads=(rrow[mp],), writes=(rrow2[mp],))
                    kb.emit("act", lambda s, mp=mp: s.activation(
                        out=rrow[mp][:, :], in_=rrow2[mp][:, :], func=AF.Sqrt),
                        reads=(rrow2[mp],), writes=(rrow[mp],))

                def post_b(m):
                    _, _, ps, dstT = mats[m]
                    mp = m % 2
                    kb.emit("pe", lambda t, mp=mp: t.matmul(
                        ps_misc[0][:, 0:512], onesf_row[:, :], rrow[mp][0:1, :],
                        start=True, stop=True),
                        reads=(onesf_row, rrow[mp]), writes=(ps_misc[0],))
                    kb.emit("act", lambda s, mp=mp: s.activation(
                        out=rrb[mp][:, :], in_=ps_misc[0][:, 0:512], func=AF.Copy),
                        reads=(ps_misc[0],), writes=(rrb[mp],))
                    kb.emit("dve", lambda v, ps=ps, dstT=dstT, base=base,
                            mp=mp: v.tensor_tensor(
                        out=dstT[:, base:base + BLK], in0=ps[:, :],
                        in1=rrb[mp][:, :], op=OP.mult),
                        reads=(ps, rrb[mp]), writes=(dstT,))

                def v_mm(tt):
                    ps = ps_q[tt]
                    for i in range(16):
                        kb.emit("pe", lambda t, tt=tt, i=i, ps=ps: t.matmul(
                            ps[:, 0:256],
                            xTb[:, i * BLK + tt * 128:i * BLK + (tt + 1) * 128],
                            wvT[:, i * 256:(i + 1) * 256],
                            start=(i == 0), stop=False),
                            reads=(xTb, wvT), writes=(ps,))
                    kb.emit("pe", lambda t, tt=tt, ps=ps, par=par: t.matmul(
                        ps[:, 0:256], murow[par][0:1, tt * 128:(tt + 1) * 128],
                        negrs[0:1, 2 * DH:2 * DH + 256],
                        start=False, stop=True),
                        reads=(murow[par], negrs), writes=(ps,))
                    gidx = blk * 4 + tt
                    kb.emit("dve", lambda v, tt=tt, ps=ps, gidx=gidx,
                            par=par: v.tensor_scalar_mul(
                        out=vj[:, gidx * 257:gidx * 257 + 256], in0=ps[:, 0:256],
                        scalar1=r4[par][:, tt:tt + 1]),
                        reads=(ps, r4[par]), writes=(vj,))

                npts = 4 if blk < 4 else 12

                def slot_point(p):
                    if p >= npts:
                        return
                    k = blk * 4 + p if blk < 4 else 16 + (blk - 4) * 12 + p
                    if k + 4 < len(TSLOTS):
                        tern_prep(k + 4)
                    tern_trans(k)

                mats_mm(0)
                slot_point(0)
                slot_point(1)
                sq_stat(0)
                mats_mm(1)
                murow_mms()
                rank1(0)
                rank1(1)
                post_a(0)
                slot_point(2)
                slot_point(3)
                sq_stat(1)
                if blk + 1 < NBLK:
                    xload(blk + 1, halves=(1,))
                mats_mm(2)
                rank1(2)
                post_a(1)
                post_b(0)
                slot_point(4)
                slot_point(5)
                sq_stat(2)
                mats_mm(3)
                rank1(3)
                post_a(2)
                post_b(1)
                slot_point(6)
                slot_point(7)
                sq_stat(3)
                r4_chain()
                v_mm(0)
                post_a(3)
                post_b(2)
                slot_point(8)
                v_mm(1)
                post_b(3)
                slot_point(9)
                v_mm(2)
                slot_point(10)
                slot_point(11)
                v_mm(3)

            xload(0)
            for kk in range(4):
                tern_prep(kk)
            for blk in range(3):
                q_block(blk)
            s_scalar_chain()
            for blk in range(3, 16):
                q_block(blk)
            gvin_keys = tuple((cc_gv_in, (g_, h_)) for g_ in range(32)
                              for h_ in range(2))
            oin_keys = tuple((cc_o_in, (i_, j8_)) for i_ in range(16)
                             for j8_ in range(2))
            kb.emit("qcc", lambda g: g.collective_compute(
                "AllGather", OP.bypass, replica_groups=[list(range(NCORES))],
                ins=[cc_o_in.ap()], outs=[cc_o_out.ap()]),
                reads=oin_keys, writes=(cc_o_out,))
            kb.emit("qcc", lambda g: g.collective_compute(
                "AllGather", OP.bypass, replica_groups=[list(range(NCORES))],
                ins=[cc_gv_in.ap()], outs=[cc_gv_out.ap()]),
                reads=gvin_keys, writes=(cc_gv_out,))

            tap("q1T", q1T[:, :], [128, BT], BF16, reads=(q1T,))
            tap("q2T", q2T[:, :], [128, BT], BF16, reads=(q2T,))
            tap("k1T", k1T[:, :], [128, BT], BF16, reads=(k1T,))
            tap("k2T", k2T[:, :], [128, BT], BF16, reads=(k2T,))
            tap("vj", vj[:, :], [128, 64 * 257], BF16, reads=(vj,))
            tap("gv_in", cc_gv_in.ap(), [DH, GV], BF16, reads=gvin_keys)
            tap("o_in", cc_o_in.ap(), [TOK, C], BF16, reads=oin_keys)

        kb.barrier()
        QB2 = 512
        with ExitStack() as sta:
            masks = [kb.sbuf(f"mask{i}", [128, 512], BF16, scope=sta)
                     for i in range(4)]
            iotat2 = kb.sbuf("iotat2", [128, 512], F32, scope=sta)
            kb.emit("pool", lambda g: g.iota(
                iotat2[:, :], pattern=[[1, 512]], base=0,
                channel_multiplier=-1,
                allow_small_or_imprecise_dtypes=True),
                writes=(iotat2,))
            for mj in range(4):
                kb.emit("dve", lambda v, mj=mj: v.tensor_single_scalar(
                    out=masks[mj][:, :], in_=iotat2[:, 0:512],
                    scalar=float(mj * 128), op=OP.is_ge),
                    reads=(iotat2,), writes=(masks[mj],))
            ystage = kb.sbuf("ystage", [128, 2 * T], BF16, scope=sta)
            e_sb = [kb.sbuf(f"e_sb{i}", [128, QB2], BF16, scope=sta)
                    for i in range(3)]
            rd_row = [kb.sbuf(f"rd_row{i}", [1, QB2], F32, scope=sta)
                      for i in range(2)]
            rd_b = [kb.sbuf(f"rd_b{i}", [128, QB2], F32, scope=sta)
                    for i in range(2)]
            z_sb = [kb.sbuf(f"z_sb{i}", [128, QB2], F32, scope=sta)
                    for i in range(2)]
            _t2buf = [kb.sbuf(f"t2buf{i}", [128, QB2], F32, scope=sta)
                      for i in range(2)]
            ps_sc = [kb.psum(f"ps_sc{i}", [128, 512], F32, scope=sta)
                     for i in range(3)]
            ps_avt = [kb.psum(f"ps_avt{i}", [128, 512], F32, scope=sta)
                      for i in range(4)]
            ps_den = kb.psum("ps_den", [128, 512], F32, scope=sta)

            for seq in range(2):
                sbase = seq * T
                for qb in range(T // QB2):
                    q0 = qb * QB2
                    n_kt = 4 * (qb + 1)
                    steps = [(a, kt) for a in range(2) for kt in range(n_kt)]

                    def emit_sc(idx, q0=q0, sbase=sbase, n_kt=n_kt, steps=steps):
                        a, kt = steps[idx]
                        qT, kT = ((q1T, k1T), (q2T, k2T))[a]
                        sc = ps_sc[idx % 3]
                        e = e_sb[idx % 3]
                        kb.emit("pe", lambda t, kT=kT, qT=qT, sc=sc, kt=kt,
                                q0=q0, sbase=sbase: t.matmul(
                            sc[:, :],
                            kT[:, sbase + kt * 128:sbase + (kt + 1) * 128],
                            qT[:, sbase + q0:sbase + q0 + QB2],
                            start=True, stop=True),
                            reads=(kT, qT), writes=(sc,))
                        kb.emit("act", lambda s, sc=sc, e=e: s.activation(
                            out=e[:, :], in_=sc[:, :], func=AF.Exp,
                            scale=ISCALE),
                            reads=(sc,), writes=(e,))
                        dj = kt - 4 * qb
                        if dj >= 0:
                            kb.emit("dve", lambda v, e=e, dj=dj: v.tensor_tensor(
                                out=e[:, :], in0=e[:, :], in1=masks[dj][:, :],
                                op=OP.mult),
                                reads=(e, masks[dj]), writes=(e,))

                    def emit_av(idx, n_kt=n_kt, steps=steps, seq=seq):
                        a, kt = steps[idx]
                        e = e_sb[idx % 3]
                        g = seq * 32 + kt
                        for half in range(2):
                            av = ps_avt[a * 2 + half]
                            kb.emit("pe", lambda t, e=e, half=half, av=av,
                                    g=g, kt=kt, n_kt=n_kt: t.matmul(
                                av[:, :],
                                vj[:, g * 257 + half * 128:g * 257 + (half + 1) * 128],
                                e[:, :],
                                start=(kt == 0), stop=(kt == n_kt - 1)),
                                reads=(e, vj), writes=(av,))
                        drow = ps_den[32 * a:32 * a + 1, :]
                        kb.emit("pe", lambda t, e=e, drow=drow, kt=kt,
                                n_kt=n_kt: t.matmul(
                            drow, ones_colb[:, 0:1], e[:, :],
                            start=(kt == 0), stop=(kt == n_kt - 1)),
                            reads=(e, ones_colb), writes=(ps_den,))
                        if kt == n_kt - 1:
                            kb.emit("dve", lambda v, a=a, drow=drow: v.reciprocal(
                                out=rd_row[a][:, :], in_=drow),
                                reads=(ps_den,), writes=(rd_row[a],))

                    emit_sc(0)
                    emit_sc(1)
                    for idx in range(len(steps)):
                        if idx + 2 < len(steps):
                            emit_sc(idx + 2)
                        emit_av(idx)
                    kb.emit("dve", lambda v: v.tensor_scalar_mul(
                        out=rd_row[1][:, :], in0=rd_row[1][:, :],
                        scalar1=neglam[0:1, 0:1]),
                        reads=(rd_row[1], neglam), writes=(rd_row[1],))
                    for a in range(2):
                        kb.emit("pe", lambda t, a=a: t.matmul(
                            ps_sc[a][:, :], onesf_row[:, :], rd_row[a][:, :],
                            start=True, stop=True),
                            reads=(onesf_row, rd_row[a]), writes=(ps_sc[a],))
                        kb.emit("act", lambda s, a=a: s.activation(
                            out=rd_b[a][:, :], in_=ps_sc[a][:, :], func=AF.Copy),
                            reads=(ps_sc[a],), writes=(rd_b[a],))
                    for half in range(2):
                        kb.emit("dve", lambda v, half=half: v.tensor_tensor(
                            out=z_sb[half][:, :], in0=ps_avt[half][:, :],
                            in1=rd_b[0][:, :], op=OP.mult),
                            reads=(ps_avt[half], rd_b[0]), writes=(z_sb[half],))
                        kb.emit("dve", lambda v, half=half: v.tensor_tensor(
                            out=_t2buf[half][:, :],
                            in0=ps_avt[2 + half][:, :], in1=rd_b[1][:, :],
                            op=OP.mult),
                            reads=(ps_avt[2 + half], rd_b[1]),
                            writes=(_t2buf[half],))
                        kb.emit("dve", lambda v, half=half, q0=q0, seq=seq: v.tensor_tensor(
                            out=ystage[:, half * T + q0:half * T + q0 + QB2],
                            in0=z_sb[half][:, :], in1=_t2buf[half][:, :],
                            op=OP.add),
                            reads=(z_sb[half], _t2buf[half]), writes=(ystage,))
                for hf in range(2):
                    kb.dma("qsp",
                           cc_y_in[seq].ap()[hf * 128:(hf + 1) * 128, :],
                           ystage[:, hf * T:(hf + 1) * T],
                           reads=(ystage,), writes=((cc_y_in[seq], hf),))
                kb.emit("qcc", lambda g, seq=seq: g.collective_compute(
                    "AllGather", OP.bypass, replica_groups=[list(range(NCORES))],
                    ins=[cc_y_in[seq].ap()],
                    outs=[cc_y_ag.ap()[seq * C:(seq + 1) * C, :]]),
                    reads=((cc_y_in[seq], 0), (cc_y_in[seq], 1)),
                    writes=((cc_y_ag, seq),))

            tap("y_in0", cc_y_in[0].ap(), [DH, T], BF16,
                reads=((cc_y_in[0], 0), (cc_y_in[0], 1)))
            tap("y_in1", cc_y_in[1].ap(), [DH, T], BF16,
                reads=((cc_y_in[1], 0), (cc_y_in[1], 1)))
        st_qa.close()
        kb.barrier()

        hqT = kb.sbuf("hqT", [128, 16 * TOK], BF16)
        with ExitStack() as stp:
            nw_sb = kb.sbuf("nw_sb", [1, C], F32, scope=stp)
            kb.dma("qsp", nw_sb[:, :], nw.ap(), reads=(nw,), writes=(nw_sb,))
            ps_o = [kb.psum(f"ps_o{i}", [128, 512], F32, scope=stp) for i in range(6)]
            ps_t2 = [kb.psum(f"ps_t2_{i}", [128, 1024], BF16, scope=stp)
                     for i in range(2)]

            nwb = kb.sbuf("nwb", [128, C], F32, scope=stp)
            for cs in range(4):
                kb.emit("pe", lambda t, cs=cs: t.matmul(
                    ps_o[0][:, :], onesf_row[:, :],
                    nw_sb[0:1, cs * 512:(cs + 1) * 512],
                    start=True, stop=True),
                    reads=(onesf_row, nw_sb), writes=(ps_o[0],))
                kb.emit("act", lambda s, cs=cs: s.activation(
                    out=nwb[:, cs * 512:(cs + 1) * 512], in_=ps_o[0][:, :],
                    func=AF.Copy),
                    reads=(ps_o[0],), writes=(nwb,))
            woT = kb.sbuf("woT", [128, 16 * C], BF16, scope=stp)
            y_loc = kb.sbuf("y_loc", [128, 16 * TOK], BF16, scope=stp)
            x1t = [kb.sbuf(f"x1t{i}", [128, C], F32, scope=stp) for i in range(2)]
            _xres1 = kb.sbuf("xres_s", [128, C], F32, scope=stp)
            ht = [kb.sbuf(f"ht{i}", [128, C], F32, scope=stp) for i in range(2)]
            hqt = [kb.sbuf(f"hqt{i}", [128, C], BF16, scope=stp) for i in range(2)]
            p2dump = kb.sbuf("p2dump", [128, C], BF16, scope=stp)
            mus4 = [kb.sbuf(f"mus4_{i}", [128, 4], F32, scope=stp)
                    for i in range(2)]
            cols = [[kb.sbuf(f"p2c{p}_{i}", [128, 1], F32, scope=stp)
                     for i in range(6)] for p in range(2)]

            cid_cell = {}
            cid_sb = kb.sbuf("cid_sb", [1, 2], mybir.dt.int32, scope=stp)
            _cid_mark = kb.emit(
                "qgp", lambda eng: eng.dma_start(out=cid_sb[:, :], in_=cid_in.ap()),
                reads=(), writes=(cid_sb,))

            def _load_cid(eng, m=_cid_mark, cell=cid_cell):
                eng.wait_ge(kb.sem[m[0]], m[1])
                cell["row"] = eng.value_load(cid_sb[0:1, 0:1])
                cell["col"] = eng.value_load(cid_sb[0:1, 1:2])
            kb.streams["pool"].append(_load_cid)
            for i in range(16):
                def _yld(eng, i=i, cell=cid_cell):
                    base = cc_y_ag.ap()[bass.ds(cell["row"], 2048),
                                        bass.ts(cell["col"], TOK)]
                    return eng.dma_start(
                        out=y_loc[:, i * TOK:(i + 1) * TOK],
                        in_=base[i * 128:(i + 1) * 128, :])
                kb.emit("qgp", _yld, reads=((cc_y_ag, 0), (cc_y_ag, 1)),
                        writes=((y_loc, i),))

            tap("y_loc", y_loc[:, :], [128, 16 * TOK], BF16,
                reads=tuple((y_loc, i_) for i_ in range(16)))

            for i in range(16):
                rdk = tuple((wo_t_dram, (j_, i // 4)) for j_ in range(16))
                kb.dma("qsp" if i % 2 == 0 else "qact",
                       woT[:, i * 2048:(i + 1) * 2048],
                       wo_t_dram.ap()[i * 128:(i + 1) * 128, :],
                       reads=rdk, writes=((woT, i),))

            def wo_mms(tt):
                par = tt % 2
                x1 = x1t[par]
                cl = cols[par]
                for cs in range(4):
                    ps = ps_o[(tt * 4 + cs) % 6]
                    for i in range(16):
                        kb.emit("pe", lambda t, tt=tt, cs=cs, i=i, ps=ps: t.matmul(
                            ps[:, :],
                            y_loc[:, i * TOK + tt * 128:i * TOK + (tt + 1) * 128],
                            woT[:, i * 2048 + cs * 512:i * 2048 + (cs + 1) * 512],
                            start=(i == 0), stop=(i == 15)),
                            reads=((y_loc, i), (woT, i)), writes=(ps,))
                    kb.emit("dve", lambda v, cs=cs, ps=ps, x1=x1,
                            par=par: v.scalar_tensor_tensor(
                        out=x1[:, cs * 512:(cs + 1) * 512], in0=ps[:, :],
                        scalar=1.0 - LAMBDA_INIT,
                        in1=_xres1[:, cs * 512:(cs + 1) * 512],
                        op0=OP.mult, op1=OP.add,
                        accum_out=mus4[par][:, cs:cs + 1]),
                        reads=(ps, (_xres1, cs)), writes=(x1, mus4[par]))
                kb.emit("act", lambda s, x1=x1, cl=cl: s.activation(
                    out=p2dump[:, :], in_=x1[:, :], func=AF.Square,
                    accum_out=cl[1][:, :]),
                    reads=(x1,), writes=(p2dump, cl[1]))

            def ln_a(tt):
                par = tt % 2
                x1 = x1t[par]
                h = ht[par]
                hq = hqt[par]
                cl = cols[par]
                kb.emit("dve", lambda v, cl=cl, par=par: v.reduce_sum(
                    out=cl[0][:, :], in_=mus4[par][:, 0:4], axis=AX.X),
                    reads=(mus4[par],), writes=(cl[0],))
                kb.emit("dve", lambda v, cl=cl: v.tensor_scalar_mul(
                    out=cl[0][:, :], in0=cl[0][:, :], scalar1=1.0 / C),
                    reads=(cl[0],), writes=(cl[0],))
                kb.emit("dve", lambda v, cl=cl: v.tensor_tensor(
                    out=cl[2][:, :], in0=cl[0][:, :], in1=cl[0][:, :],
                    op=OP.mult),
                    reads=(cl[0],), writes=(cl[2],))
                kb.emit("dve", lambda v, cl=cl: v.tensor_scalar(
                    out=cl[1][:, :], in0=cl[1][:, :], scalar1=1.0 / C,
                    scalar2=cl[2][:, 0:1], op0=OP.mult, op1=OP.subtract),
                    reads=(cl[1], cl[2]), writes=(cl[1],))
                m4 = mus4[par]
                kb.emit("dve", lambda v, cl=cl, m4=m4: v.tensor_scalar_add(
                    out=m4[:, 0:1], in0=cl[1][:, :], scalar1=LN_EPS),
                    reads=(cl[1],), writes=(m4,))
                kb.emit("dve", lambda v, cl=cl, m4=m4: v.tensor_scalar(
                    out=m4[:, 1:2], in0=cl[1][:, :], scalar1=1.0 + FFN_EPS,
                    scalar2=FFN_EPS * LN_EPS, op0=OP.mult, op1=OP.add),
                    reads=(cl[1],), writes=(m4,))
                kb.emit("dve", lambda v, m4=m4: v.reciprocal(
                    out=m4[:, 2:4], in_=m4[:, 0:2]),
                    reads=(m4,), writes=(m4,))
                kb.emit("act", lambda s, m4=m4: s.activation(
                    out=m4[:, 0:2], in_=m4[:, 2:4], func=AF.Sqrt),
                    reads=(m4,), writes=(m4,))
                kb.emit("dve", lambda v, x1=x1, h=h, cl=cl, m4=m4: v.tensor_scalar(
                    out=h[:, :], in0=x1[:, :], scalar1=cl[0][:, 0:1],
                    scalar2=m4[:, 1:2], op0=OP.subtract, op1=OP.mult),
                    reads=(x1, cl[0], m4), writes=(h,))
                kb.emit("dve", lambda v, cl=cl, m4=m4: v.tensor_scalar_add(
                    out=cl[5][:, :], in0=m4[:, 0:1], scalar1=1.0),
                    reads=(m4,), writes=(cl[5],))
                kb.emit("dve", lambda v, cl=cl, m4=m4: v.tensor_scalar(
                    out=cl[2][:, :], in0=m4[:, 0:1], scalar1=cl[0][:, 0:1],
                    scalar2=-1.0, op0=OP.mult, op1=OP.mult),
                    reads=(m4, cl[0]), writes=(cl[2],))
                kb.emit("dve", lambda v, h=h: v.tensor_tensor(
                    out=h[:, :], in0=h[:, :], in1=nwb[:, :], op=OP.mult),
                    reads=(h, nwb), writes=(h,))
                kb.emit("dve", lambda v, h=h, cl=cl: v.tensor_reduce(
                    out=cl[0][:, :], in_=h[:, :], axis=AX.X, op=OP.max,
                    apply_absolute_value=True),
                    reads=(h,), writes=(cl[0],))
                kb.emit("dve", lambda v, cl=cl: v.tensor_scalar_max(
                    out=cl[0][:, :], in0=cl[0][:, :], scalar1=1e-5),
                    reads=(cl[0],), writes=(cl[0],))
                kb.emit("dve", lambda v, tt=tt, cl=cl: v.tensor_tensor(
                    out=alph1[:, tt:tt + 1], in0=cl[0][:, 0:1],
                    in1=sgv127_col[:, 0:1], op=OP.mult),
                    reads=(cl[0], sgv127_col), writes=(alph1,))
                kb.emit("dve", lambda v, cl=cl: v.reciprocal(
                    out=cl[1][:, :], in_=cl[0][:, :]),
                    reads=(cl[0],), writes=(cl[1],))
                kb.emit("dve", lambda v, cl=cl: v.tensor_scalar_mul(
                    out=cl[1][:, :], in0=cl[1][:, :], scalar1=QB),
                    reads=(cl[1],), writes=(cl[1],))
                kb.emit("act", lambda s, h=h, cl=cl: s.activation(
                    out=h[:, :], in_=h[:, :], func=AF.Copy,
                    scale=cl[1][:, 0:1], bias=MAGIC),
                    reads=(h, cl[1]), writes=(h,))
                kb.emit("act", lambda s, h=h, hq=hq: s.activation(
                    out=hq[:, :], in_=h[:, :], func=AF.Copy, scale=1.0,
                    bias=-MAGIC),
                    reads=(h,), writes=(hq,))
                kb.emit("dve", lambda v, x1=x1, cl=cl: v.tensor_scalar(
                    out=x1[:, :], in0=x1[:, :], scalar1=cl[5][:, 0:1],
                    scalar2=cl[2][:, 0:1], op0=OP.mult, op1=OP.add),
                    reads=(x1, cl[5], cl[2]), writes=(x1,))
                kb.dma("qact", r2_dram.ap()[tt * 128:(tt + 1) * 128, :],
                       x1[:, :], reads=(x1,), writes=((r2_dram, tt),))

            def ln_b(tt):
                par = tt % 2
                hq = hqt[par]
                pst = ps_t2[par]
                for i4 in range(4):
                    for i1 in range(4):
                        i = i4 * 4 + i1
                        kb.emit("pe", lambda t, i=i, i1=i1, hq=hq, pst=pst: t.transpose(
                            pst[:, i1 * 128:(i1 + 1) * 128],
                            hq[:, i * 128:(i + 1) * 128], identb[:, :]),
                            reads=(hq, identb), writes=(pst,))
                    kb.emit("act", lambda s, i4=i4, tt=tt, pst=pst: s.activation(
                        out=hqT[:, tt * 2048 + i4 * 512:tt * 2048 + (i4 + 1) * 512],
                        in_=pst[:, 0:512], func=AF.Copy),
                        reads=(pst,), writes=(hqT,))

            for rr in range(9):
                if rr < 8:
                    for cs in range(4):
                        kb.dma("qgp", _xres1[:, cs * 512:(cs + 1) * 512],
                               x_mine.ap()[rr * 128:(rr + 1) * 128,
                                           cs * 512:(cs + 1) * 512],
                               reads=(x_mine,), writes=((_xres1, cs),))
                if 0 <= rr - 1 < 8:
                    ln_a(rr - 1)
                if rr < 8:
                    wo_mms(rr)
                if 0 <= rr - 1 < 8:
                    ln_b(rr - 1)

            tap("x1_r2", r2_dram.ap(), [TOK, C], F32,
                reads=tuple((r2_dram, t_) for t_ in range(8)))
            tap("hqT", hqT[:, :], [128, 16 * TOK], BF16, reads=(hqT,))

        kb.barrier()
        with ExitStack() as stf:
            rhs_gv = [kb.sbuf(f"rhs_gv{i}", [128, 16 * 512], BF16, scope=stf)
                      for i in range(4)]
            sg_sb = kb.sbuf("sg_sb", [128, 512], F32, scope=stf)
            h2_sb = [kb.sbuf(f"h2_sb{i}", [128, 512], F32, scope=stf)
                     for i in range(2)]
            mcol = kb.sbuf("mcol", [128, 1], F32, scope=stf)
            ps_f = [kb.psum(f"ps_f{i}", [128, 512], F32, scope=stf) for i in range(4)]

            kb.emit("dve", lambda v: v.memset(m2acc[:, :], 0.0), writes=(m2acc,))
            gv_rr = cc_gv_out.ap().rearrange("(i p) o -> i p o", p=128)
            for sp in range(16):
                bufv = rhs_gv[(sp % 2) * 2]
                bufg = rhs_gv[(sp % 2) * 2 + 1]
                for half, osl, buf in ((0, sp, bufv), (1, sp + 16, bufg)):
                    for i in range(16):
                        kb.dma("qsp", buf[:, i * 512:(i + 1) * 512],
                               gv_rr[i, :, osl * 512:(osl + 1) * 512],
                               reads=(cc_gv_out,), writes=((buf, i),))
                for tt in range(8):
                    psv = ps_f[tt % 2]
                    psg = ps_f[2 + tt % 2]
                    for i in range(16):
                        kb.emit("pe", lambda t, tt=tt, i=i, psv=psv, bufv=bufv: t.matmul(
                            psv[:, :],
                            hqT[:, tt * 2048 + i * 128:tt * 2048 + (i + 1) * 128],
                            bufv[:, i * 512:(i + 1) * 512],
                            start=(i == 0), stop=(i == 15)),
                            reads=(hqT, (bufv, i)), writes=(psv,))
                    for i in range(16):
                        kb.emit("pe", lambda t, tt=tt, i=i, psg=psg, bufg=bufg: t.matmul(
                            psg[:, :],
                            hqT[:, tt * 2048 + i * 128:tt * 2048 + (i + 1) * 128],
                            bufg[:, i * 512:(i + 1) * 512],
                            start=(i == 0), stop=(i == 15)),
                            reads=(hqT, (bufg, i)), writes=(psg,))
                    kb.emit("act", lambda s, tt=tt, psg=psg: s.activation(
                        out=sg_sb[:, :], in_=psg[:, :], func=AF.Silu,
                        scale=alph1[:, tt:tt + 1]),
                        reads=(psg, alph1), writes=(sg_sb,))
                    h2b = h2_sb[tt % 2]
                    kb.emit("dve", lambda v, tt=tt, psv=psv, h2b=h2b: v.scalar_tensor_tensor(
                        out=h2b[:, :], in0=psv[:, :], scalar=alph1[:, tt:tt + 1],
                        in1=sg_sb[:, :], op0=OP.mult, op1=OP.mult),
                        reads=(psv, alph1, sg_sb), writes=(h2b,))
                    kb.emit("dve", lambda v, h2b=h2b: v.tensor_reduce(
                        out=mcol[:, :], in_=h2b[:, :], axis=AX.X, op=OP.max,
                        apply_absolute_value=True),
                        reads=(h2b,), writes=(mcol,))
                    kb.emit("dve", lambda v, tt=tt: v.tensor_tensor(
                        out=m2acc[:, tt:tt + 1], in0=m2acc[:, tt:tt + 1],
                        in1=mcol[:, 0:1], op=OP.max),
                        reads=(m2acc, mcol), writes=(m2acc,))
                    kb.dma("qgp",
                           h2_dram.ap()[tt * 128:(tt + 1) * 128,
                                        sp * 512:(sp + 1) * 512],
                           h2b[:, :], reads=(h2b,), writes=((h2_dram, (tt, sp)),))
        kb.barrier()
        for tt in range(8):
            kb.emit("dve", lambda v, tt=tt: v.tensor_scalar_max(
                out=m2acc[:, tt:tt + 1], in0=m2acc[:, tt:tt + 1], scalar1=1e-5),
                reads=(m2acc,), writes=(m2acc,))
            kb.emit("dve", lambda v, tt=tt: v.tensor_tensor(
                out=alph2[:, tt:tt + 1], in0=m2acc[:, tt:tt + 1],
                in1=sout127_col[:, 0:1], op=OP.mult),
                reads=(m2acc, sout127_col), writes=(alph2,))
            kb.emit("dve", lambda v, tt=tt: v.reciprocal(
                out=m2s[:, tt:tt + 1], in_=m2acc[:, tt:tt + 1]),
                reads=(m2acc,), writes=(m2s,))
            kb.emit("dve", lambda v, tt=tt: v.tensor_scalar_mul(
                out=m2s[:, tt:tt + 1], in0=m2s[:, tt:tt + 1], scalar1=QB),
                reads=(m2s,), writes=(m2s,))

        hq2T = kb.sbuf("hq2T", [128, 8 * HID], BF16)
        with ExitStack() as stf2:
            h2_rb = [kb.sbuf(f"h2_rb{i}", [128, 2048], F32, scope=stf2)
                     for i in range(2)]
            h2_q = [kb.sbuf(f"h2_q{i}", [128, 2048], BF16, scope=stf2)
                    for i in range(2)]
            ps_t3 = [kb.psum(f"ps_t3{i}", [128, 1024], BF16, scope=stf2)
                     for i in range(2)]
            for tt in range(8):
                for s4 in range(4):
                    buf = h2_rb[s4 % 2]
                    hq = h2_q[s4 % 2]
                    kb.dma("qsp", buf[:, :],
                           h2_dram.ap()[tt * 128:(tt + 1) * 128,
                                        s4 * 2048:(s4 + 1) * 2048],
                           reads=tuple((h2_dram, (tt, s4 * 4 + k_)) for k_ in range(4)),
                           writes=(buf,))
                    kb.emit("dve", lambda v, buf=buf, tt=tt: v.tensor_scalar(
                        out=buf[:, :], in0=buf[:, :], scalar1=m2s[:, tt:tt + 1],
                        scalar2=MAGIC, op0=OP.mult, op1=OP.add),
                        reads=(buf, m2s), writes=(buf,))
                    kb.emit("dve", lambda v, buf=buf, hq=hq: v.tensor_scalar_add(
                        out=hq[:, :], in0=buf[:, :], scalar1=-MAGIC),
                        reads=(buf,), writes=(hq,))
                    for half in range(2):
                        pst = ps_t3[half]
                        for j in range(8):
                            i = half * 8 + j
                            kb.emit("pe", lambda t, i=i, j=j, hq=hq, pst=pst: t.transpose(
                                pst[:, j * 128:(j + 1) * 128],
                                hq[:, i * 128:(i + 1) * 128], identb[:, :]),
                                reads=(hq, identb), writes=(pst,))
                        kb.emit("dve", lambda v, tt=tt, s4=s4, half=half, pst=pst: v.tensor_copy(
                            out=hq2T[:, tt * HID + s4 * 2048 + half * 1024:
                                     tt * HID + s4 * 2048 + (half + 1) * 1024],
                            in_=pst[:, :]),
                            reads=(pst,), writes=(hq2T,))
        kb.barrier()
        with ExitStack() as stf3:
            rhs_o = [kb.sbuf(f"rhs_o{i}", [128, 2048], BF16, scope=stf3)
                     for i in range(3)]
            r2t = [kb.sbuf(f"r2t{i}", [128, 512], F32, scope=stf3) for i in range(2)]
            o_sb = [kb.sbuf(f"o_sb{i}", [128, 512], F32, scope=stf3)
                    for i in range(2)]
            ps_f3 = [kb.psum(f"ps_f3{i}", [128, 512], F32, scope=stf3)
                     for i in range(8)]
            for cs in range(4):
                for g4 in range(16):
                    buf = rhs_o[g4 % 3]
                    src = cc_o_out.ap()[g4 * 512:(g4 + 1) * 512,
                                        cs * 512:(cs + 1) * 512]
                    src = src.rearrange("(c p) n -> p c n", p=128)
                    kb.dma("qsp", buf[:, :].rearrange("p (c n) -> p c n", c=4),
                           src, reads=(cc_o_out,), writes=(buf,))
                    for tt in range(8):
                        for kc in range(4):
                            k = g4 * 4 + kc
                            kb.emit("pe", lambda t, tt=tt, kc=kc, k=k, buf=buf: t.matmul(
                                ps_f3[tt][:, :],
                                hq2T[:, tt * HID + k * 128:tt * HID + (k + 1) * 128],
                                buf[:, kc * 512:(kc + 1) * 512],
                                start=(k == 0), stop=(k == 63)),
                                reads=(hq2T, buf), writes=(ps_f3[tt],))
                for tt in range(8):
                    rb = r2t[tt % 2]
                    ob = o_sb[tt % 2]
                    kb.dma("qact", rb[:, :],
                           r2_dram.ap()[tt * 128:(tt + 1) * 128,
                                        cs * 512:(cs + 1) * 512],
                           reads=((r2_dram, tt),), writes=(rb,))
                    kb.emit("dve", lambda v, tt=tt, rb=rb, ob=ob: v.scalar_tensor_tensor(
                        out=ob[:, :], in0=ps_f3[tt][:, :], scalar=alph2[:, tt:tt + 1],
                        in1=rb[:, :], op0=OP.mult, op1=OP.add),
                        reads=(ps_f3[tt], alph2, rb), writes=(ob,))
                    kb.dma("qact",
                           out.ap()[tt * 128:(tt + 1) * 128, cs * 512:(cs + 1) * 512],
                           ob[:, :], reads=(ob,), writes=((out, (tt, cs)),))

        kb.wait_all("sp")
        kb.finalize()
    return nc


def kernel(L0_attn_x, L0_attn_wq, L0_attn_wk, L0_attn_wv, L0_attn_wo,
           L0_attn_lq1, L0_attn_lk1, L0_attn_lq2, L0_attn_lk2,
           L0_ffn_norm_w, L0_ffn_w_gv, L0_ffn_w_out):
    xf = np.ascontiguousarray(np.asarray(L0_attn_x, dtype=np.float32).reshape(BT, C))
    wqf = np.asarray(L0_attn_wq, dtype=np.float32)
    wkf = np.asarray(L0_attn_wk, dtype=np.float32)
    wvf = np.asarray(L0_attn_wv, dtype=np.float32)
    wof = np.ascontiguousarray(np.asarray(L0_attn_wo, dtype=np.float32))
    lam = np.ascontiguousarray(np.concatenate([
        np.asarray(L0_attn_lq1, dtype=np.float32),
        np.asarray(L0_attn_lk1, dtype=np.float32),
        np.asarray(L0_attn_lq2, dtype=np.float32),
        np.asarray(L0_attn_lk2, dtype=np.float32)]).reshape(1, 4 * DH))
    nwf = np.ascontiguousarray(np.asarray(L0_ffn_norm_w, dtype=np.float32).reshape(1, C))
    wgvf = np.asarray(L0_ffn_w_gv, dtype=np.float32)
    woutf = np.asarray(L0_ffn_w_out, dtype=np.float32)

    in_maps = []
    for c in range(NCORES):
        in_maps.append({
            "x": xf,
            "x_mine": np.ascontiguousarray(xf[c * TOK:(c + 1) * TOK, :]),
            "wq": np.ascontiguousarray(wqf[c * DH:(c + 1) * DH, :]),
            "wk": np.ascontiguousarray(wkf[c * DH:(c + 1) * DH, :]),
            "wv": np.ascontiguousarray(wvf[c * DH:(c + 1) * DH, :]),
            "wo": wof,
            "lam_in": lam,
            "nw": nwf,
            "wgv_sl": np.ascontiguousarray(wgvf[:, c * DH:(c + 1) * DH]),
            "wout_sl": np.ascontiguousarray(woutf[:, c * TOK:(c + 1) * TOK]),
            "cid": np.array([[(c // 4) * 2048, c % 4]], dtype=np.int32),
        })
    with allow_ici_all_to_all():
        nc = build_program()
        res = run_bass_kernel_spmd(nc, in_maps, core_ids=list(range(NCORES)))
    full = np.concatenate([np.asarray(res.results[c]["out"]) for c in range(NCORES)],
                          axis=0)
    return full.reshape(B, T, C).astype(np.float32)
